# version 07
# baseline (speedup 1.0000x reference)
import contextlib
import numpy as np
import ml_dtypes
import concourse.bass as bass
import concourse.mybir as mybir
from concourse.bass_utils import run_bass_kernel_spmd
from concourse.alu_op_type import AluOpType
from concourse.aot_env import allow_ici_all_to_all

F32 = mybir.dt.float32
BF16 = mybir.dt.bfloat16
I32 = mybir.dt.int32
ACT = mybir.ActivationFunctionType
AOP = AluOpType

CFG = dict(NC=8, TF=16384, H=2048, I=8192, NST=16, RK=64, TC=256, NG=8)


def build_nc(cfg, dbg=False, no_ffn=False, settle=0):
    NC = cfg["NC"]; TF = cfg["TF"]; H = cfg["H"]; I = cfg["I"]
    NST = cfg["NST"]; RK = cfg["RK"]; TC = cfg["TC"]; NG = cfg["NG"]
    T = TF // NC
    TT = T // 128
    HT = H // 128
    DSL = H // NC
    NDT = DSL // 128
    IT = I // 128
    NCH = TF // TC
    CPB = T // TC
    NGR = NST // NG
    HLF = H // 2
    K96 = RK + 2 * NST
    T4 = T // 512

    nc = bass.Bass("TRN2", target_bir_lowering=False,
                   detect_race_conditions=False)

    x_tok = nc.dram_tensor("x_tok", [T, H], F32, kind="ExternalInput").ap()
    xT = nc.dram_tensor("xT", [H, T], F32, kind="ExternalInput").ap()
    wxT = nc.dram_tensor("wxT", [H, K96], F32, kind="ExternalInput").ap()
    wdtT = nc.dram_tensor("wdtT", [RK, DSL], F32, kind="ExternalInput").ap()
    bdt2 = nc.dram_tensor("bdt2", [128, NDT], F32, kind="ExternalInput").ap()
    alog2 = nc.dram_tensor("alog2", [128, NDT * NST], F32, kind="ExternalInput").ap()
    dvec2 = nc.dram_tensor("dvec2", [128, NDT], F32, kind="ExternalInput").ap()
    gwT_h = nc.dram_tensor("gwT", [IT, 128, HT, 128], I32, kind="ExternalInput").ap()
    uwT_h = nc.dram_tensor("uwT", [IT, 128, HT, 128], I32, kind="ExternalInput").ap()
    dwT_h = nc.dram_tensor("dwT", [I, H], I32, kind="ExternalInput").ap()
    gsc2 = nc.dram_tensor("gsc2", [128, IT], F32, kind="ExternalInput").ap()
    usc2 = nc.dram_tensor("usc2", [128, IT], F32, kind="ExternalInput").ap()
    dsc2 = nc.dram_tensor("dsc2", [128, HT], F32, kind="ExternalInput").ap()
    identb = nc.dram_tensor("identb", [128, 128], BF16, kind="ExternalInput").ap()
    identf = nc.dram_tensor("identf", [128, 128], F32, kind="ExternalInput").ap()
    outT = nc.dram_tensor("outT", [H, T], F32, kind="ExternalOutput").ap()

    xz_b = nc.dram_tensor("xz_b", [K96, T], F32).ap()
    xzg = nc.dram_tensor("xzg", [NC, K96, T], F32).ap()
    bcD = nc.dram_tensor("bcD", [2 * NST, TF], BF16).ap()
    mD = nc.dram_tensor("mD", [NDT, 128, TF], F32).ap()
    xn_b = nc.dram_tensor("xn_b", [NC, DSL, T], BF16).ap()
    xng = nc.dram_tensor("xng", [NC, DSL, T], BF16).ap()
    y_b = nc.dram_tensor("y_b", [NC, DSL, T], F32).ap()
    yg = nc.dram_tensor("yg", [NC, DSL, T], F32).ap()
    hactD = nc.dram_tensor("hactD", [TT, 128, IT, 128], BF16).ap()

    es = contextlib.ExitStack()
    with es:
        wx_f = es.enter_context(nc.sbuf_tensor("wx_f", [128, HT, K96], F32))
        wx_b = es.enter_context(nc.sbuf_tensor("wx_b", [128, HT, K96], BF16))
        wdt_f = es.enter_context(nc.sbuf_tensor("wdt_f", [RK, DSL], F32))
        wdt_b = es.enter_context(nc.sbuf_tensor("wdt_b", [RK, DSL], BF16))
        bdt_s = es.enter_context(nc.sbuf_tensor("bdt_s", [128, NDT], F32))
        nbdt = es.enter_context(nc.sbuf_tensor("nbdt", [128, NDT], F32))
        alog_s = es.enter_context(nc.sbuf_tensor("alog_s", [128, NDT * NST], F32))
        expA = es.enter_context(nc.sbuf_tensor("expA", [128, NDT * NST], F32))
        dvec_s = es.enter_context(nc.sbuf_tensor("dvec_s", [128, NDT], F32))
        gsc_s = es.enter_context(nc.sbuf_tensor("gsc_s", [128, IT], F32))
        usc_s = es.enter_context(nc.sbuf_tensor("usc_s", [128, IT], F32))
        dsc_s = es.enter_context(nc.sbuf_tensor("dsc_s", [128, HT], F32))
        idb_s = es.enter_context(nc.sbuf_tensor("idb_s", [128, 128], BF16))
        idf_s = es.enter_context(nc.sbuf_tensor("idf_s", [128, 128], F32))
        st_s = es.enter_context(nc.sbuf_tensor("st_s", [128, 8], F32))
        init_s = es.enter_context(nc.sbuf_tensor("init_s", [128, NDT, NST], BF16))

        o_xnT = 0
        sz_xnT = HT * T // 2
        p0 = o_xnT + sz_xnT
        o_xbuf = p0
        o_sq = p0 + 2 * H
        o_xnst = p0 + 3 * H
        o_xzst = p0
        o_drawf = p0 + T
        o_drawb = p0 + 3 * T
        o_bcstf = p0 + 4 * T
        o_bcstb = p0 + 5 * T
        o_sgst = p0 + 5 * T + T // 2
        o_mst = o_sgst + 512
        p0_end = o_mst + NDT * 2 * 512
        sc = p0
        o_m = sc
        o_xnsl = o_m + NDT * 2 * TC
        o_BC = o_xnsl + NDT * TC
        o_Abar = o_BC + 2 * NST * TC
        o_Bx = o_Abar + 4 * NST * TC // 2
        o_zf = o_Bx + NDT * NG * TC // 2
        o_dx = o_zf + NST * TC // 2
        o_h = o_dx + TC // 2
        o_y = o_h + NDT * NG * TC // 2
        o_yf = o_y + TC
        sc_end = o_yf + NDT * 2 * TC
        fa = max(sc_end, p0_end)
        o_gwI = fa
        o_uwI = fa + HT * 128
        o_gwB = o_uwI + HT * 128
        o_uwB = o_gwB + HT * 64
        o_silu = o_uwB + HT * 64
        o_hact = o_silu + T // 2
        fa_end = o_hact + T // 2
        o_wdH = 0
        o_wdI = o_wdH + IT * HLF // 2
        o_hsb = o_wdI + HLF
        o_dnst = o_hsb + IT * 128
        o_fin = o_dnst + 2 * HLF
        o_outst = o_fin + 8 * 128
        o_xTb = o_outst + 8 * 128
        o_yrb = o_xTb + 8 * 128
        fb_end = o_yrb + 8 * 128
        AR = max(fa_end, fb_end)
        arena = es.enter_context(nc.sbuf_tensor("arena", [128, AR], F32))
        arA = arena.ap()
        arB = arena.ap().bitcast(BF16)
        arI = arena.ap().bitcast(I32)

        def vA(o, n):
            return arA[:, o:o + n]

        def vB(o, nb):
            return arB[:, 2 * o:2 * o + nb]

        def vI(o, n):
            return arI[:, o:o + n]

        xnT = vB(o_xnT, HT * T).rearrange("p (h t) -> p h t", h=HT)

        ps_a = es.enter_context(nc.psum_tensor("ps_a", [128, 2048], F32))
        ps_b = es.enter_context(nc.psum_tensor("ps_b", [128, 2048], F32))
        psaA = ps_a.ap()
        psbA = ps_b.ap()

        sem_names = ["qS", "qC", "qG", "qO", "cV", "cS", "cT", "cC"]
        sems = {n: es.enter_context(nc.semaphore(n)) for n in sem_names}
        block = es.enter_context(nc.Block())

        streams = {k: [] for k in ("sync", "vector", "scalar", "tensor", "gpsimd")}
        cnt = {n: 0 for n in sem_names}

        def inc(sem, n=1):
            cnt[sem] += n
            return cnt[sem]

        def emit(eng, waits, fn, then=None):
            w = [(s, v) for (s, v) in waits if v and v > 0]

            def th(e, w=w, fn=fn, then=then):
                for (s, v) in w:
                    e.wait_ge(sems[s], v)
                ins = fn(e)
                if then is not None:
                    ins.then_inc(sems[then[0]], then[1])
            streams[eng].append(th)

        def dma(eng, qsem, out_ap, in_ap, waits=()):
            tgt = inc(qsem, 16)
            emit(eng, waits,
                 lambda e, o=out_ap, i=in_ap: e.dma_start(out=o, in_=i),
                 then=(qsem, 16))
            return tgt

        t_small = 0
        for out_ap, in_ap in [
            (wx_f[:], wxT.rearrange("(h p) c -> p h c", p=128)),
            (wdt_f[:], wdtT), (bdt_s[:], bdt2),
            (alog_s[:], alog2), (dvec_s[:], dvec2),
            (gsc_s[:], gsc2), (usc_s[:], usc2), (dsc_s[:], dsc2),
            (idb_s[:], identb), (idf_s[:], identf),
        ]:
            t_small = dma("sync", "qS", out_ap, in_ap)

        def _prep(e):
            e.activation(out=wx_b[:], in_=wx_f[:], func=ACT.Copy)
            e.activation(out=wdt_b[:], in_=wdt_f[:], func=ACT.Copy)
            e.activation(out=nbdt[:], in_=bdt_s[:], func=ACT.Copy, scale=-1.0)
            return e.activation(out=expA[:], in_=alog_s[:], func=ACT.Exp)
        emit("scalar", [("qS", t_small)], _prep, then=("cS", 1))
        c_prep = inc("cS")
        emit("vector", [], lambda e: e.memset(init_s[:], 0.0), then=("cV", 1))
        inc("cV")

        c_xnact = {}
        c_xnev = {}
        for tt in range(TT):
            buf = vA(o_xbuf + (tt % 2) * H, H)
            xload = dma("sync", "qS", buf, x_tok[tt * 128:(tt + 1) * 128, :],
                        [("cS", c_xnact.get(tt - 2, 0))])
            emit("vector", [("qS", xload)],
                 lambda e, b=buf: e.reduce_sum(st_s[:, 0:1], b,
                                               axis=mybir.AxisListType.X),
                 then=("cV", 1))
            v_sx = inc("cV")
            emit("scalar", [("qS", xload)],
                 lambda e, b=buf: e.activation(out=vA(o_sq, H), in_=b,
                                               func=ACT.Square,
                                               accum_out=st_s[:, 1:2]),
                 then=("cS", 1))
            s_ssq = inc("cS")

            def _stats(e):
                e.drain()
                e.tensor_scalar_mul(st_s[:, 2:3], st_s[:, 0:1], 1.0 / H)
                e.drain()
                e.tensor_mul(st_s[:, 3:4], st_s[:, 2:3], st_s[:, 2:3])
                e.drain()
                e.scalar_tensor_tensor(out=st_s[:, 4:5], in0=st_s[:, 1:2],
                                       scalar=1.0 / H, in1=st_s[:, 3:4],
                                       op0=AOP.mult, op1=AOP.subtract)
                e.drain()
                return e.tensor_scalar_add(st_s[:, 7:8], st_s[:, 4:5], 1e-5)
            emit("vector", [("cS", s_ssq), ("cV", v_sx)], _stats, then=("cV", 1))
            v_var = inc("cV")
            emit("scalar", [("cV", v_var)],
                 lambda e: e.activation(out=st_s[:, 3:4], in_=st_s[:, 7:8],
                                        func=ACT.Sqrt),
                 then=("cS", 1))
            s_inv = inc("cS")

            def _invnb(e):
                e.drain()
                e.reciprocal(st_s[:, 5:6], st_s[:, 3:4])
                e.drain()
                return e.scalar_tensor_tensor(out=st_s[:, 6:7], in0=st_s[:, 2:3],
                                              scalar=-1.0, in1=st_s[:, 5:6],
                                              op0=AOP.mult, op1=AOP.mult)
            emit("vector", [("cS", s_inv)], _invnb, then=("cV", 1))
            v_nb = inc("cV")
            emit("scalar", [("cV", v_nb)],
                 lambda e, b=buf: e.activation(out=vA(o_xnst, H), in_=b,
                                               func=ACT.Identity,
                                               scale=st_s[:, 5:6],
                                               bias=st_s[:, 6:7]),
                 then=("cS", 1))
            c_xnact[tt] = inc("cS")
            for g in range(2):
                pv = psaA[:, (g % 2) * 1024:(g % 2 + 1) * 1024].rearrange(
                    "p (h c) -> p h c", h=8)
                wts = [("cS", c_xnact[tt]), ("cS", c_xnev.get((tt - 1, g), 0))]

                def _tr(e, g=g, pv=pv):
                    for h8 in range(8):
                        ht = g * 8 + h8
                        ins = e.transpose(pv[:, h8],
                                          vA(o_xnst, H)[:, ht * 128:(ht + 1) * 128],
                                          idf_s[:])
                    return ins
                emit("tensor", wts, _tr, then=("cT", 1))
                t_tr = inc("cT")
                emit("scalar", [("cT", t_tr)],
                     lambda e, g=g, pv=pv, tt=tt: e.activation(
                         out=xnT[:, g * 8:(g + 1) * 8, tt * 128:(tt + 1) * 128],
                         in_=pv, func=ACT.Copy),
                     then=("cS", 1))
                c_xnev[(tt, g)] = inc("cS")
        c_xnev_all = cnt["cS"]

        g_xnb = 0
        for j in range(NC):
            g_xnb = dma("gpsimd", "qG",
                        xn_b[j].rearrange("(q p) t -> q p t", p=128)
                        .transpose([1, 0, 2]),
                        xnT[:, NDT * j:NDT * (j + 1), :],
                        [("cS", c_xnev_all)] if j == 0 else [])
        emit("gpsimd", [("qG", g_xnb)],
             lambda e: e.collective_compute(
                 "AllToAll", AOP.bypass, replica_groups=[list(range(NC))],
                 ins=[xn_b.opt()], outs=[xng.opt()]),
             then=("cC", 1))
        cc_xn = inc("cC")

        s_xz_prev = 0
        for tch in range(T4):
            def _xz(e, tch=tch):
                for ht in range(HT):
                    ins = e.matmul(psaA[0:K96, 0:512], wx_b[:, ht, :],
                                   xnT[:, ht, tch * 512:(tch + 1) * 512],
                                   start=(ht == 0), stop=(ht == HT - 1))
                return ins
            emit("tensor", [("cS", c_xnev_all), ("cS", s_xz_prev)], _xz,
                 then=("cT", 1))
            t_xz = inc("cT")
            emit("scalar", [("cT", t_xz)],
                 lambda e, tch=tch: e.activation(
                     out=vA(o_xzst, T)[0:K96, tch * 512:(tch + 1) * 512],
                     in_=psaA[0:K96, 0:512], func=ACT.Copy),
                 then=("cS", 1))
            s_xz_prev = inc("cS")
        g_xz = dma("gpsimd", "qG", xz_b, vA(o_xzst, T)[0:K96, :],
                   [("cS", s_xz_prev)])
        emit("gpsimd", [("qG", g_xz)],
             lambda e: e.collective_compute(
                 "AllGather", AOP.bypass, replica_groups=[list(range(NC))],
                 ins=[xz_b.opt()], outs=[xzg.opt()]),
             then=("cC", 1))
        cc_xz = inc("cC")

        s_bcb = 0
        for j in range(NC):
            g_bcf = dma("gpsimd", "qG", vA(o_bcstf, T)[0:2 * NST, :],
                        xzg[j, RK:K96, :],
                        ([("cC", cc_xz)] if j == 0 else []) + [("cS", s_bcb)])
            emit("scalar", [("qG", g_bcf)],
                 lambda e: e.activation(out=vB(o_bcstb, T)[0:2 * NST, :],
                                        in_=vA(o_bcstf, T)[0:2 * NST, :],
                                        func=ACT.Copy),
                 then=("cS", 1))
            s_bcb = inc("cS")
            dma("gpsimd", "qG", bcD[:, j * T:(j + 1) * T],
                vB(o_bcstb, T)[0:2 * NST, :], [("cS", s_bcb)])

        s_mst = {}
        q_mst = {}
        for j in range(NC):
            wts = [("cS", s_mst.get((j - 2, T4 - 1, NDT - 1), 0))]
            if j == 0:
                wts.append(("cC", cc_xz))
            g_dr = dma("gpsimd", "qG", vA(o_drawf + (j % 2) * T, T)[0:RK, :],
                       xzg[j, 0:RK, :], wts)
            emit("scalar", [("qG", g_dr)],
                 lambda e, j=j: e.activation(
                     out=vB(o_drawb + (j % 2) * (T // 2), T)[0:RK, :],
                     in_=vA(o_drawf + (j % 2) * T, T)[0:RK, :], func=ACT.Copy),
                 then=("cS", 1))
            s_drb = inc("cS")
            for tc4 in range(T4):
                for dt in range(NDT):
                    kprev = (j, tc4 - 1, dt) if tc4 > 0 else (j - 1, T4 - 1, dt)
                    emit("tensor",
                         [("cS", s_drb), ("cS", c_prep),
                          ("cS", s_mst.get(kprev, 0)),
                          ("qG", q_mst.get(kprev, 0))],
                         lambda e, j=j, tc4=tc4, dt=dt: e.matmul(
                             psaA[:, dt * 512:(dt + 1) * 512],
                             wdt_b[:, dt * 128:(dt + 1) * 128],
                             vB(o_drawb + (j % 2) * (T // 2), T)[0:RK,
                                 tc4 * 512:(tc4 + 1) * 512],
                             start=True, stop=True),
                         then=("cT", 1))
                    t_dl = inc("cT")

                    def _m(e, dt=dt):
                        e.activation(out=vA(o_sgst, 512),
                                     in_=psaA[:, dt * 512:(dt + 1) * 512],
                                     func=ACT.Sigmoid, scale=-1.0,
                                     bias=nbdt[:, dt:dt + 1])
                        return e.activation(out=vA(o_mst + dt * 512, 512),
                                            in_=vA(o_sgst, 512), func=ACT.Ln)
                    emit("scalar", [("cT", t_dl)], _m, then=("cS", 1))
                    s_mv = inc("cS")
                    emit("vector", [("cS", s_mv)],
                         lambda e, dt=dt: e.tensor_scalar_min(
                             vA(o_mst + (NDT + dt) * 512, 512),
                             vA(o_mst + dt * 512, 512), 0.0),
                         then=("cV", 1))
                    v_mv = inc("cV")
                    q_mst[(j, tc4, dt)] = dma(
                        "gpsimd", "qG",
                        mD[dt, :, j * T + tc4 * 512: j * T + (tc4 + 1) * 512],
                        vA(o_mst + (NDT + dt) * 512, 512), [("cV", v_mv)])
                    s_mst[(j, tc4, dt)] = s_mv
        g_mD_all = cnt["qG"]
        c_delta_done = cnt["cS"]

        scan_yf = {}
        abar_c = {}
        qbc = {}
        ydma = {}
        fA_deq = {}
        fA_mm = {}
        fA_silu = {}
        fA_hactv = {}
        fA_hdma = {}

        def scan_chunk(ch):
            pp = ch % 2
            j = ch // CPB
            tloc = (ch % CPB) * TC
            glob = ch * TC
            wts = ([("cC", cc_xn), ("qG", g_mD_all)] if ch == 0 else
                   [("cV", scan_yf.get((ch - 2, NDT - 1), 0))])
            qC_ = dma(
                "scalar", "qC",
                vB(o_BC + (ch % 2) * (NST * TC), 2 * NST * TC).rearrange(
                    "p (n t) -> p n t", n=2 * NST),
                bcD[:, glob:glob + TC].partition_broadcast(128), wts)
            qm = {}
            qx = {}
            for dt in range(NDT):
                qm[dt] = dma("scalar", "qC", vA(o_m + (dt * 2 + pp) * TC, TC),
                             mD[dt, :, glob:glob + TC])
                qx[dt] = dma("scalar", "qC",
                             vB(o_xnsl + (dt * 2 + pp) * (TC // 2), TC),
                             xng[j, dt * 128:(dt + 1) * 128, tloc:tloc + TC])
            for dt in range(NDT):
                def _abar(e, ch=ch, dt=dt, pp=pp):
                    mv = vA(o_m + (dt * 2 + pp) * TC, TC)
                    ab = vB(o_Abar + ((ch * NDT + dt) % 4) * (NST * TC // 2),
                            NST * TC)
                    for n in range(NST):
                        ins = e.activation(out=ab[:, n * TC:(n + 1) * TC],
                                           in_=mv, func=ACT.Exp,
                                           scale=expA[:, dt * NST + n:
                                                      dt * NST + n + 1])
                    return ins
                emit("scalar",
                     [("qC", qm[dt]), ("cV", scan_yf.get((ch - 2, dt), 0))],
                     _abar, then=("cS", 1))
                abar_c[(ch, dt)] = inc("cS")

                def _vec(e, ch=ch, dt=dt, pp=pp):
                    mv = vA(o_m + (dt * 2 + pp) * TC, TC)
                    xnv = vB(o_xnsl + (dt * 2 + pp) * (TC // 2), TC)
                    dxv = vB(o_dx, TC)
                    ab = vB(o_Abar + ((ch * NDT + dt) % 4) * (NST * TC // 2),
                            NST * TC)
                    hb = vB(o_h + dt * (NG * TC // 2), NG * TC)
                    BC3 = vB(o_BC + (ch % 2) * (NST * TC), 2 * NST * TC).rearrange(
                        "p (n t) -> p n t", n=2 * NST)
                    bx = vB(o_Bx + dt * (NG * TC // 2), NG * TC)
                    zf = vB(o_zf, NST * TC)
                    yv = vA(o_y, TC)
                    e.scalar_tensor_tensor(out=dxv, in0=mv, scalar=-1.0,
                                           in1=xnv, op0=AOP.mult, op1=AOP.mult)
                    e.drain()
                    e.tensor_scalar_min(ab, ab, 1.0)
                    e.drain()
                    for ngr in range(NGR):
                        n0 = ngr * NG
                        e.tensor_tensor(
                            out=bx.rearrange("p (n t) -> p n t", n=NG),
                            in0=dxv.unsqueeze(1).broadcast_to([128, NG, TC]),
                            in1=BC3[:, n0:n0 + NG, :], op=AOP.mult)
                        e.drain()
                        for n in range(NG):
                            nn = n0 + n
                            e.tensor_tensor_scan(
                                out=hb[:, n * TC:(n + 1) * TC],
                                data0=ab[:, nn * TC:(nn + 1) * TC],
                                data1=bx[:, n * TC:(n + 1) * TC],
                                initial=init_s[:, dt, nn:nn + 1],
                                op0=AOP.mult, op1=AOP.add)
                            e.drain()
                        e.tensor_tensor(
                            out=zf[:, n0 * TC:(n0 + NG) * TC].rearrange(
                                "p (n t) -> p n t", n=NG),
                            in0=hb.rearrange("p (n t) -> p n t", n=NG),
                            in1=BC3[:, NST + n0:NST + n0 + NG, :],
                            op=AOP.mult)
                        e.tensor_copy(
                            init_s[:, dt, n0:n0 + NG],
                            hb.rearrange("p (n t) -> p n t", n=NG)[:, :, TC - 1])
                    z3 = zf.rearrange("p (n t) -> p n t", n=NST)
                    b3 = bx.rearrange("p (n t) -> p n t", n=NG)
                    e.drain()
                    e.tensor_tensor(out=b3, in0=z3[:, 0:8], in1=z3[:, 8:16],
                                    op=AOP.add)
                    e.drain()
                    e.tensor_tensor(out=b3[:, 0:4], in0=b3[:, 0:4],
                                    in1=b3[:, 4:8], op=AOP.add)
                    e.drain()
                    e.tensor_tensor(out=b3[:, 0:2], in0=b3[:, 0:2],
                                    in1=b3[:, 2:4], op=AOP.add)
                    e.drain()
                    e.tensor_tensor(out=yv, in0=b3[:, 0], in1=b3[:, 1],
                                    op=AOP.add)
                    e.drain()
                    return e.scalar_tensor_tensor(
                        out=vA(o_yf + (dt * 2 + pp) * TC, TC), in0=xnv,
                        scalar=dvec_s[:, dt:dt + 1], in1=yv,
                        op0=AOP.mult, op1=AOP.add)
                emit("vector",
                     [("cS", abar_c[(ch, dt)]), ("qC", qx[dt]), ("qC", qC_),
                      ("qG", ydma.get((ch - 2, dt), 0))],
                     _vec, then=("cV", 1))
                scan_yf[(ch, dt)] = inc("cV")
                ydma[(ch, dt)] = dma(
                    "gpsimd", "qG",
                    y_b[j, dt * 128:(dt + 1) * 128, tloc:tloc + TC],
                    vA(o_yf + (dt * 2 + pp) * TC, TC),
                    [("cV", scan_yf[(ch, dt)])])

        def ffnA_iter(it):
            wts = [("cS", fA_deq.get(it - 1, 0))]
            if it == 0:
                wts.append(("cS", c_delta_done))
            dma("sync", "qS",
                vI(o_gwI, HT * 128).rearrange("p (h c) -> p h c", h=HT),
                gwT_h[it], wts)
            qu = dma("sync", "qS",
                     vI(o_uwI, HT * 128).rearrange("p (h c) -> p h c", h=HT),
                     uwT_h[it])

            def _deq(e):
                e.activation(out=vB(o_gwB, HT * 128), in_=vI(o_gwI, HT * 128),
                             func=ACT.Copy, bias=-128.0)
                return e.activation(out=vB(o_uwB, HT * 128),
                                    in_=vI(o_uwI, HT * 128),
                                    func=ACT.Copy, bias=-128.0)
            emit("scalar", [("qS", qu), ("cT", fA_mm.get(it - 1, 0))], _deq,
                 then=("cS", 1))
            fA_deq[it] = inc("cS")

            def _mm(e):
                gw = vB(o_gwB, HT * 128).rearrange("p (h c) -> p h c", h=HT)
                uw = vB(o_uwB, HT * 128).rearrange("p (h c) -> p h c", h=HT)
                for ht in range(HT):
                    for tch in range(T4):
                        ins = e.matmul(psaA[:, tch * 512:(tch + 1) * 512],
                                       gw[:, ht],
                                       xnT[:, ht, tch * 512:(tch + 1) * 512],
                                       start=(ht == 0), stop=(ht == HT - 1))
                for ht in range(HT):
                    for tch in range(T4):
                        ins = e.matmul(psbA[:, tch * 512:(tch + 1) * 512],
                                       uw[:, ht],
                                       xnT[:, ht, tch * 512:(tch + 1) * 512],
                                       start=(ht == 0), stop=(ht == HT - 1))
                return ins
            emit("tensor",
                 [("cS", fA_deq[it]), ("cS", fA_silu.get(it - 1, 0)),
                  ("cV", fA_hactv.get(it - 1, 0))],
                 _mm, then=("cT", 1))
            fA_mm[it] = inc("cT")

            def _silu(e, it=it):
                for tch in range(T4):
                    ins = e.activation(
                        out=vB(o_silu, T)[:, tch * 512:(tch + 1) * 512],
                        in_=psaA[:, tch * 512:(tch + 1) * 512],
                        func=ACT.Silu, scale=gsc_s[:, it:it + 1])
                return ins
            emit("scalar", [("cT", fA_mm[it]), ("cV", fA_hactv.get(it - 1, 0))],
                 _silu, then=("cS", 1))
            fA_silu[it] = inc("cS")

            def _hact(e, it=it):
                for tch in range(T4):
                    ins = e.scalar_tensor_tensor(
                        out=vB(o_hact, T)[:, tch * 512:(tch + 1) * 512],
                        in0=psbA[:, tch * 512:(tch + 1) * 512],
                        scalar=usc_s[:, it:it + 1],
                        in1=vB(o_silu, T)[:, tch * 512:(tch + 1) * 512],
                        op0=AOP.mult, op1=AOP.mult)
                return ins
            emit("vector",
                 [("cS", fA_silu[it]), ("qS", fA_hdma.get(it - 1, 0))],
                 _hact, then=("cV", 1))
            fA_hactv[it] = inc("cV")
            fA_hdma[it] = dma("sync", "qS",
                              hactD[:, :, it, :].transpose([1, 0, 2]),
                              vB(o_hact, T).rearrange("p (a t) -> p a t", a=TT),
                              [("cV", fA_hactv[it])])

        for k in range(max(NCH, IT)):
            if k < NCH:
                scan_chunk(k)
            if k < IT and not no_ffn:
                ffnA_iter(k)

        emit("gpsimd", [("qG", ydma[(NCH - 1, NDT - 1)])],
             lambda e: e.collective_compute(
                 "AllToAll", AOP.bypass, replica_groups=[list(range(NC))],
                 ins=[y_b.opt()], outs=[yg.opt()]),
             then=("cC", 1))
        cc_y = inc("cC")
        c_scanA_done_V = cnt["cV"]
        c_scanA_done_S = cnt["cS"]

        ygf = yg.rearrange("c (q p) t -> p (c q) t", p=128)
        xTr = xT.rearrange("(s p) t -> p s t", p=128)
        outr = outT.rearrange("(s p) t -> p s t", p=128)
        fB_deq = {}
        fB_mm = {}
        fB_dn = {}
        fB_tr = {}
        fB_fin = {}
        fB_out = {}

        def fB_mm_lin(half, tt):
            if tt >= 0:
                return fB_mm.get((half, tt), 0)
            return fB_mm.get((half - 1, TT + tt), 0)

        for half in ([] if no_ffn else range(2)):
            for it in range(IT):
                wts = [("cS", fB_deq.get((half, it - 1), 0))]
                if half == 0 and it == 0:
                    wts += [("cV", c_scanA_done_V), ("cS", c_scanA_done_S),
                            ("qS", fA_hdma[IT - 1])]
                if half == 1 and it == 0:
                    wts.append(("cT", fB_mm[(0, TT - 1)]))
                qw = dma("sync", "qS", vI(o_wdI, HLF),
                         dwT_h[it * 128:(it + 1) * 128,
                               half * HLF:(half + 1) * HLF], wts)
                emit("scalar", [("qS", qw)],
                     lambda e, it=it: e.activation(
                         out=vB(o_wdH, IT * HLF).rearrange(
                             "p (i c) -> p i c", i=IT)[:, it],
                         in_=vI(o_wdI, HLF), func=ACT.Copy, bias=-128.0),
                     then=("cS", 1))
                fB_deq[(half, it)] = inc("cS")
            for tt in range(TT):
                pp = tt % 2
                wts = [("cT", fB_mm_lin(half, tt - 2))]
                if half == 0 and tt == 0:
                    wts.append(("cS", fB_deq[(0, IT - 1)]))
                qh = dma("sync", "qS",
                         vB(o_hsb + pp * (IT * 128 // 2), IT * 128)
                         .rearrange("p (i t) -> p i t", i=IT),
                         hactD[tt], wts)
                qh_g = 0
                qxt = dma("sync", "qS",
                          vA(o_xTb, 8 * 128).rearrange("p (s t) -> p s t", s=8),
                          xTr[:, half * 8:(half + 1) * 8,
                              tt * 128:(tt + 1) * 128],
                          [("cV", fB_fin.get((half, tt - 1),
                                             fB_fin.get((half - 1, TT - 1), 0)))])
                wts = [("cV", fB_fin.get((half, tt - 1),
                                         fB_fin.get((half - 1, TT - 1), 0)))]
                if half == 0 and tt == 0:
                    wts.append(("cC", cc_y))
                qyr = dma("gpsimd", "qG",
                          vA(o_yrb, 8 * 128).rearrange("p (s t) -> p s t", s=8),
                          ygf[:, half * 8:(half + 1) * 8,
                              tt * 128:(tt + 1) * 128], wts)

                u = half * TT + tt

                def lin(d, k):
                    if tt - k >= 0:
                        return d.get((half, tt - k), 0)
                    return d.get((half - 1, TT + tt - k), 0)

                def _mmB(e, pp=pp, u=u):
                    hs = vB(o_hsb + pp * (IT * 128 // 2), IT * 128).rearrange(
                        "p (i t) -> p i t", i=IT)
                    wd = vB(o_wdH, IT * HLF).rearrange("p (i c) -> p i c", i=IT)
                    for it in range(IT):
                        for dch in range(2):
                            ins = e.matmul(
                                psaA[:, (u % 2) * 1024 + dch * 512:
                                     (u % 2) * 1024 + (dch + 1) * 512],
                                hs[:, it],
                                wd[:, it, dch * 512:(dch + 1) * 512],
                                start=(it == 0), stop=(it == IT - 1))
                    return ins
                emit("tensor",
                     [("cS", fB_deq[(half, IT - 1)]), ("qS", qh),
                      ("qG", qh_g), ("cS", lin(fB_dn, 2))],
                     _mmB, then=("cT", 1))
                fB_mm[(half, tt)] = inc("cT")
                emit("scalar",
                     [("cT", fB_mm[(half, tt)]), ("cT", lin(fB_tr, 2))],
                     lambda e, u=u: e.activation(
                         out=vA(o_dnst + (u % 2) * HLF, HLF),
                         in_=psaA[:, (u % 2) * 1024:(u % 2) * 1024 + HLF],
                         func=ACT.Copy),
                     then=("cS", 1))
                fB_dn[(half, tt)] = inc("cS")

                def _trB(e, u=u):
                    for ds in range(8):
                        ins = e.transpose(
                            psbA[:, (u % 2) * 1024:(u % 2 + 1) * 1024]
                            .rearrange("p (s c) -> p s c", s=8)[:, ds],
                            vA(o_dnst + (u % 2) * HLF, HLF)[:,
                                ds * 128:(ds + 1) * 128], idf_s[:])
                    return ins
                emit("tensor",
                     [("cS", fB_dn[(half, tt)]), ("cV", lin(fB_fin, 2))],
                     _trB, then=("cT", 1))
                fB_tr[(half, tt)] = inc("cT")

                def _fin(e, half=half, u=u):
                    trv = (psbA[:, (u % 2) * 1024:(u % 2 + 1) * 1024]
                           .rearrange("p (s c) -> p s c", s=8))
                    xtv = vA(o_xTb, 8 * 128).rearrange("p (s t) -> p s t", s=8)
                    yrv = vA(o_yrb, 8 * 128).rearrange("p (s t) -> p s t", s=8)
                    fnv = vA(o_fin, 8 * 128).rearrange("p (s t) -> p s t", s=8)
                    ov = vA(o_outst, 8 * 128).rearrange("p (s t) -> p s t", s=8)
                    for ds in range(8):
                        dtg = half * 8 + ds
                        e.scalar_tensor_tensor(
                            out=fnv[:, ds], in0=trv[:, ds],
                            scalar=dsc_s[:, dtg:dtg + 1], in1=xtv[:, ds],
                            op0=AOP.mult, op1=AOP.add)
                    e.drain()
                    for ds in range(8):
                        ins = e.tensor_add(ov[:, ds], fnv[:, ds], yrv[:, ds])
                    return ins
                emit("vector",
                     [("cT", fB_tr[(half, tt)]), ("qS", qxt), ("qG", qyr),
                      ("qO", fB_out.get((half, tt - 1),
                                        fB_out.get((half - 1, TT - 1), 0)))],
                     _fin, then=("cV", 1))
                fB_fin[(half, tt)] = inc("cV")
                fB_out[(half, tt)] = dma(
                    "sync", "qO",
                    outr[:, half * 8:(half + 1) * 8, tt * 128:(tt + 1) * 128],
                    vA(o_outst, 8 * 128).rearrange("p (s t) -> p s t", s=8),
                    [("cV", fB_fin[(half, tt)])])

        if no_ffn:
            emit("gpsimd", [("cC", cc_y)], lambda e: None)
        if dbg:
            d_ea = nc.dram_tensor("d_expA", [128, NDT * NST], F32,
                                  kind="ExternalOutput").ap()
            dma("gpsimd", "qG", d_ea, expA[:],
                [("qO", cnt["qO"]), ("qG", cnt["qG"]), ("cV", cnt["cV"]),
                 ("cS", cnt["cS"])])
            taps = [("d_xnb", xn_b), ("d_xzb", xz_b), ("d_yb", y_b),
                    ("d_xzg", xzg), ("d_mD", mD), ("d_xng", xng),
                    ("d_yg", yg), ("d_hact", hactD), ("d_bcD", bcD)]
            for nm, src_ap in taps:
                o = nc.dram_tensor(nm, list(src_ap.shape), src_ap.dtype,
                                   kind="ExternalOutput").ap()
                dma("gpsimd", "qG", o, src_ap, [])
            emit("gpsimd", [("qG", cnt["qG"])], lambda e: None)
        emit("sync", [("qO", cnt["qO"])], lambda e: None)

        with allow_ici_all_to_all():
            @block.sync
            def _(eng):
                for th in streams["sync"]:
                    th(eng)

            @block.vector
            def _(eng):
                for th in streams["vector"]:
                    th(eng)

            @block.scalar
            def _(eng):
                for th in streams["scalar"]:
                    th(eng)

            @block.tensor
            def _(eng):
                for th in streams["tensor"]:
                    th(eng)

            @block.gpsimd
            def _(eng):
                for th in streams["gpsimd"]:
                    th(eng)
    return nc


def _mk_inputs(cfg, L0_attn_x, L0_attn_W_x, L0_attn_W_dt, L0_attn_b_dt,
               L0_attn_A_log, L0_attn_D, L0_ffn_gate_w, L0_ffn_gate_scale,
               L0_ffn_up_w, L0_ffn_up_scale, L0_ffn_down_w, L0_ffn_down_scale):
    NC = cfg["NC"]; TF = cfg["TF"]; H = cfg["H"]; I = cfg["I"]
    NST = cfg["NST"]
    T = TF // NC
    DSL = H // NC
    NDT = DSL // 128
    IT = I // 128
    HT = H // 128
    f32 = np.float32
    x2 = np.asarray(L0_attn_x, f32).reshape(TF, H)
    wxT = np.ascontiguousarray(np.asarray(L0_attn_W_x, f32).T)
    wdtT_full = np.ascontiguousarray(np.asarray(L0_attn_W_dt, f32).T)
    b_dt = np.asarray(L0_attn_b_dt, f32)
    A_log = np.asarray(L0_attn_A_log, f32)
    Dv = np.asarray(L0_attn_D, f32)
    gwT = np.ascontiguousarray(
        np.asarray(L0_ffn_gate_w, np.int32).T.reshape(HT, 128, IT, 128)
        .transpose(2, 1, 0, 3))
    uwT = np.ascontiguousarray(
        np.asarray(L0_ffn_up_w, np.int32).T.reshape(HT, 128, IT, 128)
        .transpose(2, 1, 0, 3))
    dwT = np.ascontiguousarray(np.asarray(L0_ffn_down_w, np.int32).T)
    gsc2 = np.ascontiguousarray(np.asarray(L0_ffn_gate_scale, f32).reshape(IT, 128).T)
    usc2 = np.ascontiguousarray(np.asarray(L0_ffn_up_scale, f32).reshape(IT, 128).T)
    dsc2 = np.ascontiguousarray(np.asarray(L0_ffn_down_scale, f32).reshape(HT, 128).T)
    identb = np.eye(128, dtype=ml_dtypes.bfloat16)
    identf = np.eye(128, dtype=f32)
    in_maps = []
    for k in range(NC):
        sl = slice(k * DSL, (k + 1) * DSL)
        x_tok = np.ascontiguousarray(x2[k * T:(k + 1) * T])
        im = dict(
            x_tok=x_tok,
            xT=np.ascontiguousarray(x_tok.T),
            wxT=wxT,
            wdtT=np.ascontiguousarray(wdtT_full[:, sl]),
            bdt2=np.ascontiguousarray(b_dt[sl].reshape(NDT, 128).T),
            alog2=np.ascontiguousarray(
                A_log[sl].reshape(NDT, 128, NST).transpose(1, 0, 2)
                .reshape(128, NDT * NST)),
            dvec2=np.ascontiguousarray(Dv[sl].reshape(NDT, 128).T),
            gwT=gwT, uwT=uwT, dwT=dwT,
            gsc2=gsc2, usc2=usc2, dsc2=dsc2,
            identb=identb, identf=identf,
        )
        in_maps.append(im)
    return in_maps


def kernel(L0_attn_x, L0_attn_W_x, L0_attn_W_dt, L0_attn_b_dt, L0_attn_A_log,
           L0_attn_D, L0_ffn_gate_w, L0_ffn_gate_scale, L0_ffn_up_w,
           L0_ffn_up_scale, L0_ffn_down_w, L0_ffn_down_scale):
    cfg = CFG
    nc = build_nc(cfg)
    in_maps = _mk_inputs(cfg, L0_attn_x, L0_attn_W_x, L0_attn_W_dt,
                         L0_attn_b_dt, L0_attn_A_log, L0_attn_D,
                         L0_ffn_gate_w, L0_ffn_gate_scale, L0_ffn_up_w,
                         L0_ffn_up_scale, L0_ffn_down_w, L0_ffn_down_scale)
    with allow_ici_all_to_all():
        res = run_bass_kernel_spmd(nc, in_maps, core_ids=list(range(cfg["NC"])))
    TF = cfg["TF"]; H = cfg["H"]; NC = cfg["NC"]
    T = TF // NC
    out = np.empty((1, TF, H), np.float32)
    for k in range(NC):
        out[0, k * T:(k + 1) * T, :] = np.asarray(res.results[k]["outT"]).T
    return out


# version 09
# speedup vs baseline: 1.0417x; 1.0417x over previous
import contextlib
import numpy as np
import ml_dtypes
import concourse.bass as bass
import concourse.mybir as mybir
from concourse.bass_utils import run_bass_kernel_spmd
from concourse.alu_op_type import AluOpType
from concourse.aot_env import allow_ici_all_to_all

F32 = mybir.dt.float32
BF16 = mybir.dt.bfloat16
I32 = mybir.dt.int32
ACT = mybir.ActivationFunctionType
AOP = AluOpType

CFG = dict(NC=8, TF=16384, H=2048, I=8192, NST=16, RK=64, TC=256, NG=8)


def build_nc(cfg, dbg=False, no_ffn=False, settle=0):
    NC = cfg["NC"]; TF = cfg["TF"]; H = cfg["H"]; I = cfg["I"]
    NST = cfg["NST"]; RK = cfg["RK"]; TC = cfg["TC"]; NG = cfg["NG"]
    T = TF // NC
    TT = T // 128
    HT = H // 128
    DSL = H // NC
    NDT = DSL // 128
    IT = I // 128
    NCH = TF // TC
    CPB = T // TC
    NGR = NST // NG
    HLF = H // 2
    K96 = RK + 2 * NST
    T4 = T // 512

    nc = bass.Bass("TRN2", target_bir_lowering=False,
                   detect_race_conditions=False)

    x_tok = nc.dram_tensor("x_tok", [T, H], F32, kind="ExternalInput").ap()
    xT = nc.dram_tensor("xT", [H, T], F32, kind="ExternalInput").ap()
    wxT = nc.dram_tensor("wxT", [H, K96], F32, kind="ExternalInput").ap()
    wdtT = nc.dram_tensor("wdtT", [RK, DSL], F32, kind="ExternalInput").ap()
    bdt2 = nc.dram_tensor("bdt2", [128, NDT], F32, kind="ExternalInput").ap()
    alog2 = nc.dram_tensor("alog2", [128, NDT * NST], F32, kind="ExternalInput").ap()
    dvec2 = nc.dram_tensor("dvec2", [128, NDT], F32, kind="ExternalInput").ap()
    gwT_h = nc.dram_tensor("gwT", [IT, 128, HT, 128], I32, kind="ExternalInput").ap()
    uwT_h = nc.dram_tensor("uwT", [IT, 128, HT, 128], I32, kind="ExternalInput").ap()
    dwT_h = nc.dram_tensor("dwT", [I, H], I32, kind="ExternalInput").ap()
    gsc2 = nc.dram_tensor("gsc2", [128, IT], F32, kind="ExternalInput").ap()
    usc2 = nc.dram_tensor("usc2", [128, IT], F32, kind="ExternalInput").ap()
    dsc2 = nc.dram_tensor("dsc2", [128, HT], F32, kind="ExternalInput").ap()
    identb = nc.dram_tensor("identb", [128, 128], BF16, kind="ExternalInput").ap()
    identf = nc.dram_tensor("identf", [128, 128], F32, kind="ExternalInput").ap()
    outT = nc.dram_tensor("outT", [H, T], F32, kind="ExternalOutput").ap()

    xz_b = nc.dram_tensor("xz_b", [K96, T], F32).ap()
    xzg = nc.dram_tensor("xzg", [NC, K96, T], F32).ap()
    bcD = nc.dram_tensor("bcD", [2 * NST, TF], BF16).ap()
    mD = nc.dram_tensor("mD", [NDT, 128, TF], F32).ap()
    xn_b = nc.dram_tensor("xn_b", [NC, DSL, T], BF16).ap()
    xng = nc.dram_tensor("xng", [NC, DSL, T], BF16).ap()
    y_b = nc.dram_tensor("y_b", [NC, DSL, T], F32).ap()
    yg = nc.dram_tensor("yg", [NC, DSL, T], F32).ap()
    hactD = nc.dram_tensor("hactD", [TT, 128, IT, 128], BF16).ap()
    wdBF = nc.dram_tensor("wdBF", [I, H], BF16).ap()

    es = contextlib.ExitStack()
    with es:
        wx_f = es.enter_context(nc.sbuf_tensor("wx_f", [128, HT, K96], F32))
        wx_b = es.enter_context(nc.sbuf_tensor("wx_b", [128, HT, K96], BF16))
        wdt_f = es.enter_context(nc.sbuf_tensor("wdt_f", [RK, DSL], F32))
        wdt_b = es.enter_context(nc.sbuf_tensor("wdt_b", [RK, DSL], BF16))
        bdt_s = es.enter_context(nc.sbuf_tensor("bdt_s", [128, NDT], F32))
        nbdt = es.enter_context(nc.sbuf_tensor("nbdt", [128, NDT], F32))
        alog_s = es.enter_context(nc.sbuf_tensor("alog_s", [128, NDT * NST], F32))
        expA = es.enter_context(nc.sbuf_tensor("expA", [128, NDT * NST], F32))
        dvec_s = es.enter_context(nc.sbuf_tensor("dvec_s", [128, NDT], F32))
        gsc_s = es.enter_context(nc.sbuf_tensor("gsc_s", [128, IT], F32))
        usc_s = es.enter_context(nc.sbuf_tensor("usc_s", [128, IT], F32))
        dsc_s = es.enter_context(nc.sbuf_tensor("dsc_s", [128, HT], F32))
        idb_s = es.enter_context(nc.sbuf_tensor("idb_s", [128, 128], BF16))
        idf_s = es.enter_context(nc.sbuf_tensor("idf_s", [128, 128], F32))
        st_s = es.enter_context(nc.sbuf_tensor("st_s", [128, 8], F32))
        init_s = es.enter_context(nc.sbuf_tensor("init_s", [128, NDT, NST], BF16))

        o_xnT = 0
        sz_xnT = HT * T // 2
        p0 = o_xnT + sz_xnT
        o_xbuf = p0
        o_sq = p0 + 2 * H
        o_xnst = p0 + 3 * H
        o_xzst = p0
        o_drawf = p0 + T
        o_drawb = p0 + 3 * T
        o_bcstf = p0 + 4 * T
        o_bcstb = p0 + 5 * T
        o_sgst = p0 + 5 * T + T // 2
        o_mst = o_sgst + 512
        p0_end = o_mst + NDT * 2 * 512
        sc = p0
        o_m = sc
        o_xnsl = o_m + NDT * 2 * TC
        o_BC = o_xnsl + NDT * TC
        o_Abar = o_BC + 2 * NST * TC
        o_Bx = o_Abar + 4 * NST * TC // 2
        o_zf = o_Bx + NDT * NG * TC // 2
        o_dx = o_zf + NST * TC // 2
        o_h = o_dx + TC // 2
        o_y = o_h + NDT * NG * TC // 2
        o_yf = o_y + TC
        sc_end = o_yf + NDT * 2 * TC
        fa = max(sc_end, p0_end)
        o_gwI = fa
        o_uwI = fa + HT * 128
        o_gwB = o_uwI + HT * 128
        o_uwB = o_gwB + HT * 64
        o_silu = o_uwB + HT * 64
        o_hact = o_silu + T // 2
        fa_end = o_hact + T // 2
        o_wdH = 0
        o_wdI = o_wdH + IT * HLF // 2
        o_hsb = o_wdI + HLF
        o_dnst = o_hsb + IT * 128
        o_fin = o_dnst + 2 * HLF
        o_outst = o_fin + 8 * 128
        o_xTb = o_outst + 8 * 128
        o_yrb = o_xTb + 8 * 128
        fb_end = o_yrb + 8 * 128
        AR = max(fa_end, fb_end)
        arena = es.enter_context(nc.sbuf_tensor("arena", [128, AR], F32))
        arA = arena.ap()
        arB = arena.ap().bitcast(BF16)
        arI = arena.ap().bitcast(I32)

        def vA(o, n):
            return arA[:, o:o + n]

        def vB(o, nb):
            return arB[:, 2 * o:2 * o + nb]

        def vI(o, n):
            return arI[:, o:o + n]

        xnT = vB(o_xnT, HT * T).rearrange("p (h t) -> p h t", h=HT)

        ps_a = es.enter_context(nc.psum_tensor("ps_a", [128, 2048], F32))
        ps_b = es.enter_context(nc.psum_tensor("ps_b", [128, 2048], F32))
        psaA = ps_a.ap()
        psbA = ps_b.ap()

        sem_names = ["qS", "qC", "qG", "qO", "cV", "cS", "cT", "cC"]
        sems = {n: es.enter_context(nc.semaphore(n)) for n in sem_names}
        block = es.enter_context(nc.Block())

        streams = {k: [] for k in ("sync", "vector", "scalar", "tensor", "gpsimd")}
        cnt = {n: 0 for n in sem_names}

        def inc(sem, n=1):
            cnt[sem] += n
            return cnt[sem]

        def emit(eng, waits, fn, then=None):
            w = [(s, v) for (s, v) in waits if v and v > 0]

            def th(e, w=w, fn=fn, then=then):
                for (s, v) in w:
                    e.wait_ge(sems[s], v)
                ins = fn(e)
                if then is not None:
                    ins.then_inc(sems[then[0]], then[1])
            streams[eng].append(th)

        def dma(eng, qsem, out_ap, in_ap, waits=()):
            tgt = inc(qsem, 16)
            emit(eng, waits,
                 lambda e, o=out_ap, i=in_ap: e.dma_start(out=o, in_=i),
                 then=(qsem, 16))
            return tgt

        t_small = 0
        for out_ap, in_ap in [
            (wx_f[:], wxT.rearrange("(h p) c -> p h c", p=128)),
            (wdt_f[:], wdtT), (bdt_s[:], bdt2),
            (alog_s[:], alog2), (dvec_s[:], dvec2),
            (gsc_s[:], gsc2), (usc_s[:], usc2), (dsc_s[:], dsc2),
            (idb_s[:], identb), (idf_s[:], identf),
        ]:
            t_small = dma("sync", "qS", out_ap, in_ap)

        def _prep(e):
            e.activation(out=wx_b[:], in_=wx_f[:], func=ACT.Copy)
            e.activation(out=wdt_b[:], in_=wdt_f[:], func=ACT.Copy)
            e.activation(out=nbdt[:], in_=bdt_s[:], func=ACT.Copy, scale=-1.0)
            return e.activation(out=expA[:], in_=alog_s[:], func=ACT.Exp)
        emit("scalar", [("qS", t_small)], _prep, then=("cS", 1))
        c_prep = inc("cS")
        s_wdpre = {}
        q_wdst = {}
        for grp in range(IT):
            b = grp % 2
            o_in = o_gwI if b == 0 else o_uwI
            o_out = o_gwB if b == 0 else o_uwB
            qwp = dma("sync", "qS", vI(o_in, 2048),
                      dwT_h[grp * 128:(grp + 1) * 128, :],
                      [("qS", q_wdst.get(grp - 2, 0))])
            emit("scalar", [("qS", qwp), ("cS", s_wdpre.get(grp - 2, 0))],
                 lambda e, o_in=o_in, o_out=o_out: e.activation(
                     out=vB(o_out, 2048), in_=vI(o_in, 2048),
                     func=ACT.Copy, bias=-128.0),
                 then=("cS", 1))
            s_wdpre[grp] = inc("cS")
            q_wdst[grp] = dma("sync", "qS",
                              wdBF[grp * 128:(grp + 1) * 128, :],
                              vB(o_out, 2048),
                              [("cS", s_wdpre[grp])])
        q_wdpre = cnt["qS"]
        emit("vector", [], lambda e: e.memset(init_s[:], 0.0), then=("cV", 1))
        inc("cV")

        c_xnact = {}
        c_xnev = {}
        for tt in range(TT):
            buf = vA(o_xbuf + (tt % 2) * H, H)
            xload = dma("sync", "qS", buf, x_tok[tt * 128:(tt + 1) * 128, :],
                        [("cS", c_xnact.get(tt - 2, 0))])
            emit("vector", [("qS", xload)],
                 lambda e, b=buf: e.reduce_sum(st_s[:, 0:1], b,
                                               axis=mybir.AxisListType.X),
                 then=("cV", 1))
            v_sx = inc("cV")
            emit("scalar", [("qS", xload)],
                 lambda e, b=buf: e.activation(out=vA(o_sq, H), in_=b,
                                               func=ACT.Square,
                                               accum_out=st_s[:, 1:2]),
                 then=("cS", 1))
            s_ssq = inc("cS")

            def _stats(e):
                e.drain()
                e.tensor_scalar_mul(st_s[:, 2:3], st_s[:, 0:1], 1.0 / H)
                e.drain()
                e.tensor_mul(st_s[:, 3:4], st_s[:, 2:3], st_s[:, 2:3])
                e.drain()
                e.scalar_tensor_tensor(out=st_s[:, 4:5], in0=st_s[:, 1:2],
                                       scalar=1.0 / H, in1=st_s[:, 3:4],
                                       op0=AOP.mult, op1=AOP.subtract)
                e.drain()
                return e.tensor_scalar_add(st_s[:, 7:8], st_s[:, 4:5], 1e-5)
            emit("vector", [("cS", s_ssq), ("cV", v_sx)], _stats, then=("cV", 1))
            v_var = inc("cV")
            emit("scalar", [("cV", v_var)],
                 lambda e: e.activation(out=st_s[:, 3:4], in_=st_s[:, 7:8],
                                        func=ACT.Sqrt),
                 then=("cS", 1))
            s_inv = inc("cS")

            def _invnb(e):
                e.drain()
                e.reciprocal(st_s[:, 5:6], st_s[:, 3:4])
                e.drain()
                return e.scalar_tensor_tensor(out=st_s[:, 6:7], in0=st_s[:, 2:3],
                                              scalar=-1.0, in1=st_s[:, 5:6],
                                              op0=AOP.mult, op1=AOP.mult)
            emit("vector", [("cS", s_inv)], _invnb, then=("cV", 1))
            v_nb = inc("cV")
            emit("scalar", [("cV", v_nb)],
                 lambda e, b=buf: e.activation(out=vA(o_xnst, H), in_=b,
                                               func=ACT.Identity,
                                               scale=st_s[:, 5:6],
                                               bias=st_s[:, 6:7]),
                 then=("cS", 1))
            c_xnact[tt] = inc("cS")
            for g in range(2):
                pv = psaA[:, (g % 2) * 1024:(g % 2 + 1) * 1024].rearrange(
                    "p (h c) -> p h c", h=8)
                wts = [("cS", c_xnact[tt]), ("cS", c_xnev.get((tt - 1, g), 0))]

                def _tr(e, g=g, pv=pv):
                    for h8 in range(8):
                        ht = g * 8 + h8
                        ins = e.transpose(pv[:, h8],
                                          vA(o_xnst, H)[:, ht * 128:(ht + 1) * 128],
                                          idf_s[:])
                    return ins
                emit("tensor", wts, _tr, then=("cT", 1))
                t_tr = inc("cT")
                emit("scalar", [("cT", t_tr)],
                     lambda e, g=g, pv=pv, tt=tt: e.activation(
                         out=xnT[:, g * 8:(g + 1) * 8, tt * 128:(tt + 1) * 128],
                         in_=pv, func=ACT.Copy),
                     then=("cS", 1))
                c_xnev[(tt, g)] = inc("cS")
        c_xnev_all = cnt["cS"]

        g_xnb = 0
        for j in range(NC):
            g_xnb = dma("gpsimd", "qG",
                        xn_b[j].rearrange("(q p) t -> q p t", p=128)
                        .transpose([1, 0, 2]),
                        xnT[:, NDT * j:NDT * (j + 1), :],
                        [("cS", c_xnev_all)] if j == 0 else [])
        emit("gpsimd", [("qG", g_xnb)],
             lambda e: e.collective_compute(
                 "AllToAll", AOP.bypass, replica_groups=[list(range(NC))],
                 ins=[xn_b.opt()], outs=[xng.opt()]),
             then=("cC", 1))
        cc_xn = inc("cC")

        s_xz_prev = 0
        for tch in range(T4):
            def _xz(e, tch=tch):
                for ht in range(HT):
                    ins = e.matmul(psaA[0:K96, 0:512], wx_b[:, ht, :],
                                   xnT[:, ht, tch * 512:(tch + 1) * 512],
                                   start=(ht == 0), stop=(ht == HT - 1))
                return ins
            emit("tensor", [("cS", c_xnev_all), ("cS", s_xz_prev)], _xz,
                 then=("cT", 1))
            t_xz = inc("cT")
            emit("scalar", [("cT", t_xz)],
                 lambda e, tch=tch: e.activation(
                     out=vA(o_xzst, T)[0:K96, tch * 512:(tch + 1) * 512],
                     in_=psaA[0:K96, 0:512], func=ACT.Copy),
                 then=("cS", 1))
            s_xz_prev = inc("cS")
        g_xz = dma("gpsimd", "qG", xz_b, vA(o_xzst, T)[0:K96, :],
                   [("cS", s_xz_prev)])
        emit("gpsimd", [("qG", g_xz)],
             lambda e: e.collective_compute(
                 "AllGather", AOP.bypass, replica_groups=[list(range(NC))],
                 ins=[xz_b.opt()], outs=[xzg.opt()]),
             then=("cC", 1))
        cc_xz = inc("cC")

        s_bcb = 0
        for j in range(NC):
            g_bcf = dma("gpsimd", "qG", vA(o_bcstf, T)[0:2 * NST, :],
                        xzg[j, RK:K96, :],
                        ([("cC", cc_xz)] if j == 0 else []) + [("cS", s_bcb)])
            emit("scalar", [("qG", g_bcf)],
                 lambda e: e.activation(out=vB(o_bcstb, T)[0:2 * NST, :],
                                        in_=vA(o_bcstf, T)[0:2 * NST, :],
                                        func=ACT.Copy),
                 then=("cS", 1))
            s_bcb = inc("cS")
            dma("gpsimd", "qG", bcD[:, j * T:(j + 1) * T],
                vB(o_bcstb, T)[0:2 * NST, :], [("cS", s_bcb)])

        s_mst = {}
        q_mst = {}
        for j in range(NC):
            wts = [("cS", s_mst.get((j - 2, T4 - 1, NDT - 1), 0))]
            if j == 0:
                wts.append(("cC", cc_xz))
            g_dr = dma("gpsimd", "qG", vA(o_drawf + (j % 2) * T, T)[0:RK, :],
                       xzg[j, 0:RK, :], wts)
            emit("scalar", [("qG", g_dr)],
                 lambda e, j=j: e.activation(
                     out=vB(o_drawb + (j % 2) * (T // 2), T)[0:RK, :],
                     in_=vA(o_drawf + (j % 2) * T, T)[0:RK, :], func=ACT.Copy),
                 then=("cS", 1))
            s_drb = inc("cS")
            for tc4 in range(T4):
                for dt in range(NDT):
                    kprev = (j, tc4 - 1, dt) if tc4 > 0 else (j - 1, T4 - 1, dt)
                    emit("tensor",
                         [("cS", s_drb), ("cS", c_prep),
                          ("cS", s_mst.get(kprev, 0)),
                          ("qG", q_mst.get(kprev, 0))],
                         lambda e, j=j, tc4=tc4, dt=dt: e.matmul(
                             psaA[:, dt * 512:(dt + 1) * 512],
                             wdt_b[:, dt * 128:(dt + 1) * 128],
                             vB(o_drawb + (j % 2) * (T // 2), T)[0:RK,
                                 tc4 * 512:(tc4 + 1) * 512],
                             start=True, stop=True),
                         then=("cT", 1))
                    t_dl = inc("cT")

                    def _m(e, dt=dt):
                        e.activation(out=vA(o_sgst, 512),
                                     in_=psaA[:, dt * 512:(dt + 1) * 512],
                                     func=ACT.Sigmoid, scale=-1.0,
                                     bias=nbdt[:, dt:dt + 1])
                        return e.activation(out=vA(o_mst + dt * 512, 512),
                                            in_=vA(o_sgst, 512), func=ACT.Ln)
                    emit("scalar", [("cT", t_dl)], _m, then=("cS", 1))
                    s_mv = inc("cS")
                    emit("vector", [("cS", s_mv)],
                         lambda e, dt=dt: e.tensor_scalar_min(
                             vA(o_mst + (NDT + dt) * 512, 512),
                             vA(o_mst + dt * 512, 512), 0.0),
                         then=("cV", 1))
                    v_mv = inc("cV")
                    q_mst[(j, tc4, dt)] = dma(
                        "gpsimd", "qG",
                        mD[dt, :, j * T + tc4 * 512: j * T + (tc4 + 1) * 512],
                        vA(o_mst + (NDT + dt) * 512, 512), [("cV", v_mv)])
                    s_mst[(j, tc4, dt)] = s_mv
        g_mD_all = cnt["qG"]
        c_delta_done = cnt["cS"]

        scan_yf = {}
        abar_c = {}
        qbc = {}
        ydma = {}
        fA_deq = {}
        fA_mm = {}
        fA_silu = {}
        fA_hactv = {}
        fA_hdma = {}

        def scan_chunk(ch):
            pp = ch % 2
            j = ch // CPB
            tloc = (ch % CPB) * TC
            glob = ch * TC
            wts = ([("cC", cc_xn), ("qG", g_mD_all)] if ch == 0 else
                   [("cV", scan_yf.get((ch - 2, NDT - 1), 0))])
            qC_ = dma(
                "gpsimd", "qC",
                vB(o_BC + (ch % 2) * (NST * TC), 2 * NST * TC).rearrange(
                    "p (n t) -> p n t", n=2 * NST),
                bcD[:, glob:glob + TC].partition_broadcast(128), wts)
            qm = {}
            qx = {}
            for dt in range(NDT):
                qm[dt] = dma("gpsimd", "qC", vA(o_m + (dt * 2 + pp) * TC, TC),
                             mD[dt, :, glob:glob + TC])
                qx[dt] = dma("gpsimd", "qC",
                             vB(o_xnsl + (dt * 2 + pp) * (TC // 2), TC),
                             xng[j, dt * 128:(dt + 1) * 128, tloc:tloc + TC])
            for dt in range(NDT):
                def _abar(e, ch=ch, dt=dt, pp=pp):
                    mv = vA(o_m + (dt * 2 + pp) * TC, TC)
                    ab = vB(o_Abar + ((ch * NDT + dt) % 4) * (NST * TC // 2),
                            NST * TC)
                    for n in range(NST):
                        ins = e.activation(out=ab[:, n * TC:(n + 1) * TC],
                                           in_=mv, func=ACT.Exp,
                                           scale=expA[:, dt * NST + n:
                                                      dt * NST + n + 1])
                    return ins
                emit("scalar",
                     [("qC", qm[dt]), ("cV", scan_yf.get((ch - 2, dt), 0))],
                     _abar, then=("cS", 1))
                abar_c[(ch, dt)] = inc("cS")

                def _vec(e, ch=ch, dt=dt, pp=pp):
                    mv = vA(o_m + (dt * 2 + pp) * TC, TC)
                    xnv = vB(o_xnsl + (dt * 2 + pp) * (TC // 2), TC)
                    dxv = vB(o_dx, TC)
                    ab = vB(o_Abar + ((ch * NDT + dt) % 4) * (NST * TC // 2),
                            NST * TC)
                    hb = vB(o_h + dt * (NG * TC // 2), NG * TC)
                    BC3 = vB(o_BC + (ch % 2) * (NST * TC), 2 * NST * TC).rearrange(
                        "p (n t) -> p n t", n=2 * NST)
                    bx = vB(o_Bx + dt * (NG * TC // 2), NG * TC)
                    zf = vB(o_zf, NST * TC)
                    yv = vA(o_y, TC)
                    e.scalar_tensor_tensor(out=dxv, in0=mv, scalar=-1.0,
                                           in1=xnv, op0=AOP.mult, op1=AOP.mult)
                    e.tensor_scalar_min(ab, ab, 1.0)
                    e.drain()
                    for ngr in range(NGR):
                        n0 = ngr * NG
                        e.tensor_tensor(
                            out=bx.rearrange("p (n t) -> p n t", n=NG),
                            in0=dxv.unsqueeze(1).broadcast_to([128, NG, TC]),
                            in1=BC3[:, n0:n0 + NG, :], op=AOP.mult)
                        e.nop()
                        for n in range(NG):
                            nn = n0 + n
                            e.tensor_tensor_scan(
                                out=hb[:, n * TC:(n + 1) * TC],
                                data0=ab[:, nn * TC:(nn + 1) * TC],
                                data1=bx[:, n * TC:(n + 1) * TC],
                                initial=init_s[:, dt, nn:nn + 1],
                                op0=AOP.mult, op1=AOP.add)
                            e.nop()
                        e.tensor_tensor(
                            out=zf[:, n0 * TC:(n0 + NG) * TC].rearrange(
                                "p (n t) -> p n t", n=NG),
                            in0=hb.rearrange("p (n t) -> p n t", n=NG),
                            in1=BC3[:, NST + n0:NST + n0 + NG, :],
                            op=AOP.mult)
                        e.tensor_copy(
                            init_s[:, dt, n0:n0 + NG],
                            hb.rearrange("p (n t) -> p n t", n=NG)[:, :, TC - 1])
                    z3 = zf.rearrange("p (n t) -> p n t", n=NST)
                    b3 = bx.rearrange("p (n t) -> p n t", n=NG)
                    e.drain()
                    e.tensor_tensor(out=b3, in0=z3[:, 0:8], in1=z3[:, 8:16],
                                    op=AOP.add)
                    e.nop()
                    e.tensor_tensor(out=b3[:, 0:4], in0=b3[:, 0:4],
                                    in1=b3[:, 4:8], op=AOP.add)
                    e.nop()
                    e.tensor_tensor(out=b3[:, 0:2], in0=b3[:, 0:2],
                                    in1=b3[:, 2:4], op=AOP.add)
                    e.nop()
                    e.tensor_tensor(out=yv, in0=b3[:, 0], in1=b3[:, 1],
                                    op=AOP.add)
                    e.nop()
                    return e.scalar_tensor_tensor(
                        out=vA(o_yf + (dt * 2 + pp) * TC, TC), in0=xnv,
                        scalar=dvec_s[:, dt:dt + 1], in1=yv,
                        op0=AOP.mult, op1=AOP.add)
                emit("vector",
                     [("cS", abar_c[(ch, dt)]), ("qC", qx[dt]), ("qC", qC_),
                      ("qG", ydma.get((ch - 2, dt), 0))],
                     _vec, then=("cV", 1))
                scan_yf[(ch, dt)] = inc("cV")
                ydma[(ch, dt)] = dma(
                    "gpsimd", "qG",
                    y_b[j, dt * 128:(dt + 1) * 128, tloc:tloc + TC],
                    vA(o_yf + (dt * 2 + pp) * TC, TC),
                    [("cV", scan_yf[(ch, dt)])])

        def ffnA_iter(it):
            wts = [("cS", fA_deq.get(it - 1, 0))]
            if it == 0:
                wts.append(("qS", q_wdpre))
            if it == 0:
                wts.append(("cS", c_delta_done))
            dma("sync", "qS",
                vI(o_gwI, HT * 128).rearrange("p (h c) -> p h c", h=HT),
                gwT_h[it], wts)
            qu = dma("sync", "qS",
                     vI(o_uwI, HT * 128).rearrange("p (h c) -> p h c", h=HT),
                     uwT_h[it])

            def _deq(e):
                e.activation(out=vB(o_gwB, HT * 128), in_=vI(o_gwI, HT * 128),
                             func=ACT.Copy, bias=-128.0)
                return e.activation(out=vB(o_uwB, HT * 128),
                                    in_=vI(o_uwI, HT * 128),
                                    func=ACT.Copy, bias=-128.0)
            emit("scalar", [("qS", qu), ("cT", fA_mm.get(it - 1, 0))], _deq,
                 then=("cS", 1))
            fA_deq[it] = inc("cS")

            def _mm(e):
                gw = vB(o_gwB, HT * 128).rearrange("p (h c) -> p h c", h=HT)
                uw = vB(o_uwB, HT * 128).rearrange("p (h c) -> p h c", h=HT)
                for ht in range(HT):
                    for tch in range(T4):
                        ins = e.matmul(psaA[:, tch * 512:(tch + 1) * 512],
                                       gw[:, ht],
                                       xnT[:, ht, tch * 512:(tch + 1) * 512],
                                       start=(ht == 0), stop=(ht == HT - 1))
                for ht in range(HT):
                    for tch in range(T4):
                        ins = e.matmul(psbA[:, tch * 512:(tch + 1) * 512],
                                       uw[:, ht],
                                       xnT[:, ht, tch * 512:(tch + 1) * 512],
                                       start=(ht == 0), stop=(ht == HT - 1))
                return ins
            emit("tensor",
                 [("cS", fA_deq[it]), ("cS", fA_silu.get(it - 1, 0)),
                  ("cV", fA_hactv.get(it - 1, 0))],
                 _mm, then=("cT", 1))
            fA_mm[it] = inc("cT")

            def _silu(e, it=it):
                for tch in range(T4):
                    ins = e.activation(
                        out=vB(o_silu, T)[:, tch * 512:(tch + 1) * 512],
                        in_=psaA[:, tch * 512:(tch + 1) * 512],
                        func=ACT.Silu, scale=gsc_s[:, it:it + 1])
                return ins
            emit("scalar", [("cT", fA_mm[it]), ("cV", fA_hactv.get(it - 1, 0))],
                 _silu, then=("cS", 1))
            fA_silu[it] = inc("cS")

            def _hact(e, it=it):
                for tch in range(T4):
                    ins = e.scalar_tensor_tensor(
                        out=vB(o_hact, T)[:, tch * 512:(tch + 1) * 512],
                        in0=psbA[:, tch * 512:(tch + 1) * 512],
                        scalar=usc_s[:, it:it + 1],
                        in1=vB(o_silu, T)[:, tch * 512:(tch + 1) * 512],
                        op0=AOP.mult, op1=AOP.mult)
                return ins
            emit("vector",
                 [("cS", fA_silu[it]), ("qS", fA_hdma.get(it - 1, 0))],
                 _hact, then=("cV", 1))
            fA_hactv[it] = inc("cV")
            fA_hdma[it] = dma("sync", "qS",
                              hactD[:, :, it, :].transpose([1, 0, 2]),
                              vB(o_hact, T).rearrange("p (a t) -> p a t", a=TT),
                              [("cV", fA_hactv[it])])

        for k in range(max(NCH, IT)):
            if k < NCH:
                scan_chunk(k)
            if k < IT and not no_ffn:
                ffnA_iter(k)

        emit("gpsimd", [("qG", ydma[(NCH - 1, NDT - 1)])],
             lambda e: e.collective_compute(
                 "AllToAll", AOP.bypass, replica_groups=[list(range(NC))],
                 ins=[y_b.opt()], outs=[yg.opt()]),
             then=("cC", 1))
        cc_y = inc("cC")
        c_scanA_done_V = cnt["cV"]
        c_scanA_done_S = cnt["cS"]

        ygf = yg.rearrange("c (q p) t -> p (c q) t", p=128)
        xTr = xT.rearrange("(s p) t -> p s t", p=128)
        outr = outT.rearrange("(s p) t -> p s t", p=128)
        fB_deq = {}
        fB_mm = {}
        fB_dn = {}
        fB_tr = {}
        fB_fin = {}
        fB_out = {}

        def fB_mm_lin(half, tt):
            if tt >= 0:
                return fB_mm.get((half, tt), 0)
            return fB_mm.get((half - 1, TT + tt), 0)

        for half in ([] if no_ffn else range(2)):
            wts = []
            if half == 0:
                wts += [("cV", c_scanA_done_V), ("cS", c_scanA_done_S),
                        ("qS", fA_hdma[IT - 1])]
            else:
                wts.append(("cT", fB_mm[(0, TT - 1)]))
            ITS = max(1, IT // 8)
            for sub in range(IT // ITS):
                qwd = dma("sync", "qS",
                          vB(o_wdH, IT * HLF).rearrange(
                              "p (i c) -> p i c",
                              i=IT)[:, sub * ITS:(sub + 1) * ITS],
                          wdBF.rearrange("(i p) c -> p i c", p=128)[:,
                              sub * ITS:(sub + 1) * ITS,
                              half * HLF:(half + 1) * HLF],
                          wts if sub == 0 else [])
            fB_deq[(half, IT - 1)] = 0
            fB_wload = qwd
            for tt in range(TT):
                pp = tt % 2
                wts = [("cT", fB_mm_lin(half, tt - 2))]
                if tt == 0:
                    wts.append(("qS", fB_wload))
                qh = dma("sync", "qS",
                         vB(o_hsb + pp * (IT * 128 // 2), IT * 128)
                         .rearrange("p (i t) -> p i t", i=IT),
                         hactD[tt], wts)
                qh_g = 0
                qxt = dma("sync", "qS",
                          vA(o_xTb, 8 * 128).rearrange("p (s t) -> p s t", s=8),
                          xTr[:, half * 8:(half + 1) * 8,
                              tt * 128:(tt + 1) * 128],
                          [("cV", fB_fin.get((half, tt - 1),
                                             fB_fin.get((half - 1, TT - 1), 0)))])
                wts = [("cV", fB_fin.get((half, tt - 1),
                                         fB_fin.get((half - 1, TT - 1), 0)))]
                if half == 0 and tt == 0:
                    wts.append(("cC", cc_y))
                qyr = dma("gpsimd", "qG",
                          vA(o_yrb, 8 * 128).rearrange("p (s t) -> p s t", s=8),
                          ygf[:, half * 8:(half + 1) * 8,
                              tt * 128:(tt + 1) * 128], wts)

                u = half * TT + tt

                def lin(d, k):
                    if tt - k >= 0:
                        return d.get((half, tt - k), 0)
                    return d.get((half - 1, TT + tt - k), 0)

                def _mmB(e, pp=pp, u=u):
                    hs = vB(o_hsb + pp * (IT * 128 // 2), IT * 128).rearrange(
                        "p (i t) -> p i t", i=IT)
                    wd = vB(o_wdH, IT * HLF).rearrange("p (i c) -> p i c", i=IT)
                    for it in range(IT):
                        for dch in range(2):
                            ins = e.matmul(
                                psaA[:, (u % 2) * 1024 + dch * 512:
                                     (u % 2) * 1024 + (dch + 1) * 512],
                                hs[:, it],
                                wd[:, it, dch * 512:(dch + 1) * 512],
                                start=(it == 0), stop=(it == IT - 1))
                    return ins
                emit("tensor",
                     [("qS", qh), ("qG", qh_g), ("cS", lin(fB_dn, 2))],
                     _mmB, then=("cT", 1))
                fB_mm[(half, tt)] = inc("cT")
                emit("scalar",
                     [("cT", fB_mm[(half, tt)]), ("cT", lin(fB_tr, 2))],
                     lambda e, u=u: e.activation(
                         out=vA(o_dnst + (u % 2) * HLF, HLF),
                         in_=psaA[:, (u % 2) * 1024:(u % 2) * 1024 + HLF],
                         func=ACT.Copy),
                     then=("cS", 1))
                fB_dn[(half, tt)] = inc("cS")

                def _trB(e, u=u):
                    for ds in range(8):
                        ins = e.transpose(
                            psbA[:, (u % 2) * 1024:(u % 2 + 1) * 1024]
                            .rearrange("p (s c) -> p s c", s=8)[:, ds],
                            vA(o_dnst + (u % 2) * HLF, HLF)[:,
                                ds * 128:(ds + 1) * 128], idf_s[:])
                    return ins
                emit("tensor",
                     [("cS", fB_dn[(half, tt)]), ("cV", lin(fB_fin, 2))],
                     _trB, then=("cT", 1))
                fB_tr[(half, tt)] = inc("cT")

                def _fin(e, half=half, u=u):
                    trv = (psbA[:, (u % 2) * 1024:(u % 2 + 1) * 1024]
                           .rearrange("p (s c) -> p s c", s=8))
                    xtv = vA(o_xTb, 8 * 128).rearrange("p (s t) -> p s t", s=8)
                    yrv = vA(o_yrb, 8 * 128).rearrange("p (s t) -> p s t", s=8)
                    fnv = vA(o_fin, 8 * 128).rearrange("p (s t) -> p s t", s=8)
                    ov = vA(o_outst, 8 * 128).rearrange("p (s t) -> p s t", s=8)
                    for ds in range(8):
                        dtg = half * 8 + ds
                        e.scalar_tensor_tensor(
                            out=fnv[:, ds], in0=trv[:, ds],
                            scalar=dsc_s[:, dtg:dtg + 1], in1=xtv[:, ds],
                            op0=AOP.mult, op1=AOP.add)
                    e.drain()
                    for ds in range(8):
                        ins = e.tensor_add(ov[:, ds], fnv[:, ds], yrv[:, ds])
                    return ins
                emit("vector",
                     [("cT", fB_tr[(half, tt)]), ("qS", qxt), ("qG", qyr),
                      ("qO", fB_out.get((half, tt - 1),
                                        fB_out.get((half - 1, TT - 1), 0)))],
                     _fin, then=("cV", 1))
                fB_fin[(half, tt)] = inc("cV")
                fB_out[(half, tt)] = dma(
                    "sync", "qO",
                    outr[:, half * 8:(half + 1) * 8, tt * 128:(tt + 1) * 128],
                    vA(o_outst, 8 * 128).rearrange("p (s t) -> p s t", s=8),
                    [("cV", fB_fin[(half, tt)])])

        if no_ffn:
            emit("gpsimd", [("cC", cc_y)], lambda e: None)
        if dbg:
            d_ea = nc.dram_tensor("d_expA", [128, NDT * NST], F32,
                                  kind="ExternalOutput").ap()
            dma("gpsimd", "qG", d_ea, expA[:],
                [("qO", cnt["qO"]), ("qG", cnt["qG"]), ("cV", cnt["cV"]),
                 ("cS", cnt["cS"])])
            taps = [("d_wdBF", wdBF),
                    ("d_xnb", xn_b), ("d_xzb", xz_b), ("d_yb", y_b),
                    ("d_xzg", xzg), ("d_mD", mD), ("d_xng", xng),
                    ("d_yg", yg), ("d_hact", hactD), ("d_bcD", bcD)]
            for nm, src_ap in taps:
                o = nc.dram_tensor(nm, list(src_ap.shape), src_ap.dtype,
                                   kind="ExternalOutput").ap()
                dma("gpsimd", "qG", o, src_ap, [])
            emit("gpsimd", [("qG", cnt["qG"])], lambda e: None)
        emit("sync", [("qO", cnt["qO"])], lambda e: None)

        with allow_ici_all_to_all():
            @block.sync
            def _(eng):
                for th in streams["sync"]:
                    th(eng)

            @block.vector
            def _(eng):
                for th in streams["vector"]:
                    th(eng)

            @block.scalar
            def _(eng):
                for th in streams["scalar"]:
                    th(eng)

            @block.tensor
            def _(eng):
                for th in streams["tensor"]:
                    th(eng)

            @block.gpsimd
            def _(eng):
                for th in streams["gpsimd"]:
                    th(eng)
    return nc


def _mk_inputs(cfg, L0_attn_x, L0_attn_W_x, L0_attn_W_dt, L0_attn_b_dt,
               L0_attn_A_log, L0_attn_D, L0_ffn_gate_w, L0_ffn_gate_scale,
               L0_ffn_up_w, L0_ffn_up_scale, L0_ffn_down_w, L0_ffn_down_scale):
    NC = cfg["NC"]; TF = cfg["TF"]; H = cfg["H"]; I = cfg["I"]
    NST = cfg["NST"]
    T = TF // NC
    DSL = H // NC
    NDT = DSL // 128
    IT = I // 128
    HT = H // 128
    f32 = np.float32
    x2 = np.asarray(L0_attn_x, f32).reshape(TF, H)
    wxT = np.ascontiguousarray(np.asarray(L0_attn_W_x, f32).T)
    wdtT_full = np.ascontiguousarray(np.asarray(L0_attn_W_dt, f32).T)
    b_dt = np.asarray(L0_attn_b_dt, f32)
    A_log = np.asarray(L0_attn_A_log, f32)
    Dv = np.asarray(L0_attn_D, f32)
    gwT = np.ascontiguousarray(
        np.asarray(L0_ffn_gate_w, np.int32).T.reshape(HT, 128, IT, 128)
        .transpose(2, 1, 0, 3))
    uwT = np.ascontiguousarray(
        np.asarray(L0_ffn_up_w, np.int32).T.reshape(HT, 128, IT, 128)
        .transpose(2, 1, 0, 3))
    dwT = np.ascontiguousarray(np.asarray(L0_ffn_down_w, np.int32).T)
    gsc2 = np.ascontiguousarray(np.asarray(L0_ffn_gate_scale, f32).reshape(IT, 128).T)
    usc2 = np.ascontiguousarray(np.asarray(L0_ffn_up_scale, f32).reshape(IT, 128).T)
    dsc2 = np.ascontiguousarray(np.asarray(L0_ffn_down_scale, f32).reshape(HT, 128).T)
    identb = np.eye(128, dtype=ml_dtypes.bfloat16)
    identf = np.eye(128, dtype=f32)
    in_maps = []
    for k in range(NC):
        sl = slice(k * DSL, (k + 1) * DSL)
        x_tok = np.ascontiguousarray(x2[k * T:(k + 1) * T])
        im = dict(
            x_tok=x_tok,
            xT=np.ascontiguousarray(x_tok.T),
            wxT=wxT,
            wdtT=np.ascontiguousarray(wdtT_full[:, sl]),
            bdt2=np.ascontiguousarray(b_dt[sl].reshape(NDT, 128).T),
            alog2=np.ascontiguousarray(
                A_log[sl].reshape(NDT, 128, NST).transpose(1, 0, 2)
                .reshape(128, NDT * NST)),
            dvec2=np.ascontiguousarray(Dv[sl].reshape(NDT, 128).T),
            gwT=gwT, uwT=uwT, dwT=dwT,
            gsc2=gsc2, usc2=usc2, dsc2=dsc2,
            identb=identb, identf=identf,
        )
        in_maps.append(im)
    return in_maps


def kernel(L0_attn_x, L0_attn_W_x, L0_attn_W_dt, L0_attn_b_dt, L0_attn_A_log,
           L0_attn_D, L0_ffn_gate_w, L0_ffn_gate_scale, L0_ffn_up_w,
           L0_ffn_up_scale, L0_ffn_down_w, L0_ffn_down_scale):
    cfg = CFG
    nc = build_nc(cfg)
    in_maps = _mk_inputs(cfg, L0_attn_x, L0_attn_W_x, L0_attn_W_dt,
                         L0_attn_b_dt, L0_attn_A_log, L0_attn_D,
                         L0_ffn_gate_w, L0_ffn_gate_scale, L0_ffn_up_w,
                         L0_ffn_up_scale, L0_ffn_down_w, L0_ffn_down_scale)
    with allow_ici_all_to_all():
        res = run_bass_kernel_spmd(nc, in_maps, core_ids=list(range(cfg["NC"])))
    TF = cfg["TF"]; H = cfg["H"]; NC = cfg["NC"]
    T = TF // NC
    out = np.empty((1, TF, H), np.float32)
    for k in range(NC):
        out[0, k * T:(k + 1) * T, :] = np.asarray(res.results[k]["outT"]).T
    return out


# version 10 (final)
# speedup vs baseline: 1.0631x; 1.0205x over previous
import contextlib
import numpy as np
import ml_dtypes
import concourse.bass as bass
import concourse.mybir as mybir
from concourse.bass_utils import run_bass_kernel_spmd
from concourse.alu_op_type import AluOpType
from concourse.aot_env import allow_ici_all_to_all

F32 = mybir.dt.float32
BF16 = mybir.dt.bfloat16
I32 = mybir.dt.int32
ACT = mybir.ActivationFunctionType
AOP = AluOpType

CFG = dict(NC=8, TF=16384, H=2048, I=8192, NST=16, RK=64, TC=256, NG=8)


def build_nc(cfg, dbg=False, no_ffn=False, settle=0):
    NC = cfg["NC"]; TF = cfg["TF"]; H = cfg["H"]; I = cfg["I"]
    NST = cfg["NST"]; RK = cfg["RK"]; TC = cfg["TC"]; NG = cfg["NG"]
    T = TF // NC
    TT = T // 128
    HT = H // 128
    DSL = H // NC
    NDT = DSL // 128
    IT = I // 128
    NCH = TF // TC
    CPB = T // TC
    NGR = NST // NG
    HLF = H // 2
    K96 = RK + 2 * NST
    T4 = T // 512

    nc = bass.Bass("TRN2", target_bir_lowering=False,
                   detect_race_conditions=False)

    x_tok = nc.dram_tensor("x_tok", [T, H], F32, kind="ExternalInput").ap()
    xT = nc.dram_tensor("xT", [H, T], F32, kind="ExternalInput").ap()
    wxT = nc.dram_tensor("wxT", [H, K96], F32, kind="ExternalInput").ap()
    wdtT = nc.dram_tensor("wdtT", [RK, DSL], F32, kind="ExternalInput").ap()
    bdt2 = nc.dram_tensor("bdt2", [128, NDT], F32, kind="ExternalInput").ap()
    alog2 = nc.dram_tensor("alog2", [128, NDT * NST], F32, kind="ExternalInput").ap()
    dvec2 = nc.dram_tensor("dvec2", [128, NDT], F32, kind="ExternalInput").ap()
    gwT_h = nc.dram_tensor("gwT", [IT, 128, HT, 128], I32, kind="ExternalInput").ap()
    uwT_h = nc.dram_tensor("uwT", [IT, 128, HT, 128], I32, kind="ExternalInput").ap()
    dwT_h = nc.dram_tensor("dwT", [I, H], I32, kind="ExternalInput").ap()
    gsc2 = nc.dram_tensor("gsc2", [128, IT], F32, kind="ExternalInput").ap()
    usc2 = nc.dram_tensor("usc2", [128, IT], F32, kind="ExternalInput").ap()
    dsc2 = nc.dram_tensor("dsc2", [128, HT], F32, kind="ExternalInput").ap()
    identb = nc.dram_tensor("identb", [128, 128], BF16, kind="ExternalInput").ap()
    identf = nc.dram_tensor("identf", [128, 128], F32, kind="ExternalInput").ap()
    outT = nc.dram_tensor("outT", [H, T], F32, kind="ExternalOutput").ap()

    xz_b = nc.dram_tensor("xz_b", [K96, T], F32).ap()
    xzg = nc.dram_tensor("xzg", [NC, K96, T], F32).ap()
    bcD = nc.dram_tensor("bcD", [2 * NST, TF], BF16).ap()
    mD = nc.dram_tensor("mD", [NDT, 128, TF], F32).ap()
    xn_b = nc.dram_tensor("xn_b", [NC, DSL, T], BF16).ap()
    xng = nc.dram_tensor("xng", [NC, DSL, T], BF16).ap()
    y_b = nc.dram_tensor("y_b", [NC, DSL, T], F32).ap()
    yg = nc.dram_tensor("yg", [NC, DSL, T], F32).ap()
    hactD = nc.dram_tensor("hactD", [TT, 128, IT, 128], BF16).ap()
    wdBF = nc.dram_tensor("wdBF", [I, H], BF16).ap()

    es = contextlib.ExitStack()
    with es:
        wx_f = es.enter_context(nc.sbuf_tensor("wx_f", [128, HT, K96], F32))
        wx_b = es.enter_context(nc.sbuf_tensor("wx_b", [128, HT, K96], BF16))
        wdt_f = es.enter_context(nc.sbuf_tensor("wdt_f", [RK, DSL], F32))
        wdt_b = es.enter_context(nc.sbuf_tensor("wdt_b", [RK, DSL], BF16))
        bdt_s = es.enter_context(nc.sbuf_tensor("bdt_s", [128, NDT], F32))
        nbdt = es.enter_context(nc.sbuf_tensor("nbdt", [128, NDT], F32))
        alog_s = es.enter_context(nc.sbuf_tensor("alog_s", [128, NDT * NST], F32))
        expA = es.enter_context(nc.sbuf_tensor("expA", [128, NDT * NST], F32))
        dvec_s = es.enter_context(nc.sbuf_tensor("dvec_s", [128, NDT], F32))
        gsc_s = es.enter_context(nc.sbuf_tensor("gsc_s", [128, IT], F32))
        usc_s = es.enter_context(nc.sbuf_tensor("usc_s", [128, IT], F32))
        dsc_s = es.enter_context(nc.sbuf_tensor("dsc_s", [128, HT], F32))
        idb_s = es.enter_context(nc.sbuf_tensor("idb_s", [128, 128], BF16))
        idf_s = es.enter_context(nc.sbuf_tensor("idf_s", [128, 128], F32))
        st_s = es.enter_context(nc.sbuf_tensor("st_s", [128, 8], F32))
        init_s = es.enter_context(nc.sbuf_tensor("init_s", [128, NDT, NST], BF16))

        o_xnT = 0
        sz_xnT = HT * T // 2
        p0 = o_xnT + sz_xnT
        o_xbuf = p0
        o_sq = p0 + 2 * H
        o_xnst = p0 + 3 * H
        o_xzst = p0
        o_drawf = p0 + T
        o_drawb = p0 + 3 * T
        o_bcstf = p0 + 4 * T
        o_bcstb = p0 + 5 * T
        o_sgst = p0 + 5 * T + T // 2
        o_mst = o_sgst + 512
        p0_end = o_mst + NDT * 2 * 512
        sc = p0
        o_m = sc
        o_xnsl = o_m + NDT * 2 * TC
        o_BC = o_xnsl + NDT * TC
        o_Abar = o_BC + 2 * NST * TC
        o_Bx = o_Abar + 4 * NST * TC // 2
        o_zf = o_Bx + NDT * NG * TC // 2
        o_dx = o_zf + NST * TC // 2
        o_h = o_dx + TC // 2
        o_y = o_h + NDT * NG * TC // 2
        o_yf = o_y + TC
        sc_end = o_yf + NDT * 2 * TC
        fa = max(sc_end, p0_end)
        o_gwI = fa
        o_uwI = fa + HT * 128
        o_gwB = o_uwI + HT * 128
        o_uwB = o_gwB + HT * 64
        o_silu = o_uwB + HT * 64
        o_hact = o_silu + T // 2
        fa_end = o_hact + T // 2
        o_wdH = 0
        o_wdI = o_wdH + IT * HLF // 2
        o_hsb = o_wdI + HLF
        o_dnst = o_hsb + IT * 128
        o_fin = o_dnst + 2 * HLF
        o_outst = o_fin + 8 * 128
        o_xTb = o_outst + 8 * 128
        o_yrb = o_xTb + 8 * 128
        fb_end = o_yrb + 8 * 128
        AR = max(fa_end, fb_end)
        arena = es.enter_context(nc.sbuf_tensor("arena", [128, AR], F32))
        arA = arena.ap()
        arB = arena.ap().bitcast(BF16)
        arI = arena.ap().bitcast(I32)

        def vA(o, n):
            return arA[:, o:o + n]

        def vB(o, nb):
            return arB[:, 2 * o:2 * o + nb]

        def vI(o, n):
            return arI[:, o:o + n]

        xnT = vB(o_xnT, HT * T).rearrange("p (h t) -> p h t", h=HT)

        ps_a = es.enter_context(nc.psum_tensor("ps_a", [128, 2048], F32))
        ps_b = es.enter_context(nc.psum_tensor("ps_b", [128, 2048], F32))
        psaA = ps_a.ap()
        psbA = ps_b.ap()

        sem_names = ["qS", "qC", "qG", "qO", "cV", "cS", "cT", "cC"]
        sems = {n: es.enter_context(nc.semaphore(n)) for n in sem_names}
        block = es.enter_context(nc.Block())

        streams = {k: [] for k in ("sync", "vector", "scalar", "tensor", "gpsimd")}
        cnt = {n: 0 for n in sem_names}

        def inc(sem, n=1):
            cnt[sem] += n
            return cnt[sem]

        def emit(eng, waits, fn, then=None):
            w = [(s, v) for (s, v) in waits if v and v > 0]

            def th(e, w=w, fn=fn, then=then):
                for (s, v) in w:
                    e.wait_ge(sems[s], v)
                ins = fn(e)
                if then is not None:
                    ins.then_inc(sems[then[0]], then[1])
            streams[eng].append(th)

        def dma(eng, qsem, out_ap, in_ap, waits=()):
            tgt = inc(qsem, 16)
            emit(eng, waits,
                 lambda e, o=out_ap, i=in_ap: e.dma_start(out=o, in_=i),
                 then=(qsem, 16))
            return tgt

        t_small = 0
        for out_ap, in_ap in [
            (wx_f[:], wxT.rearrange("(h p) c -> p h c", p=128)),
            (wdt_f[:], wdtT), (bdt_s[:], bdt2),
            (alog_s[:], alog2), (dvec_s[:], dvec2),
            (gsc_s[:], gsc2), (usc_s[:], usc2), (dsc_s[:], dsc2),
            (idb_s[:], identb), (idf_s[:], identf),
        ]:
            t_small = dma("sync", "qS", out_ap, in_ap)

        def _prep(e):
            e.activation(out=wx_b[:], in_=wx_f[:], func=ACT.Copy)
            e.activation(out=wdt_b[:], in_=wdt_f[:], func=ACT.Copy)
            e.activation(out=nbdt[:], in_=bdt_s[:], func=ACT.Copy, scale=-1.0)
            return e.activation(out=expA[:], in_=alog_s[:], func=ACT.Exp)
        emit("scalar", [("qS", t_small)], _prep, then=("cS", 1))
        c_prep = inc("cS")
        s_wdpre = {}
        q_wdst = {}
        for grp in range(IT):
            b = grp % 2
            o_in = o_gwI if b == 0 else o_uwI
            o_out = o_gwB if b == 0 else o_uwB
            qwp = dma("sync", "qS", vI(o_in, 2048),
                      dwT_h[grp * 128:(grp + 1) * 128, :],
                      [("qS", q_wdst.get(grp - 2, 0))])
            emit("scalar", [("qS", qwp), ("cS", s_wdpre.get(grp - 2, 0))],
                 lambda e, o_in=o_in, o_out=o_out: e.activation(
                     out=vB(o_out, 2048), in_=vI(o_in, 2048),
                     func=ACT.Copy, bias=-128.0),
                 then=("cS", 1))
            s_wdpre[grp] = inc("cS")
            q_wdst[grp] = dma("sync", "qS",
                              wdBF[grp * 128:(grp + 1) * 128, :],
                              vB(o_out, 2048),
                              [("cS", s_wdpre[grp])])
        q_wdpre = cnt["qS"]
        emit("vector", [], lambda e: e.memset(init_s[:], 0.0), then=("cV", 1))
        inc("cV")

        c_xnact = {}
        c_xnev = {}
        for tt in range(TT):
            buf = vA(o_xbuf + (tt % 2) * H, H)
            xload = dma("sync", "qS", buf, x_tok[tt * 128:(tt + 1) * 128, :],
                        [("cS", c_xnact.get(tt - 2, 0))])
            emit("vector", [("qS", xload)],
                 lambda e, b=buf: e.reduce_sum(st_s[:, 0:1], b,
                                               axis=mybir.AxisListType.X),
                 then=("cV", 1))
            v_sx = inc("cV")
            emit("scalar", [("qS", xload)],
                 lambda e, b=buf: e.activation(out=vA(o_sq, H), in_=b,
                                               func=ACT.Square,
                                               accum_out=st_s[:, 1:2]),
                 then=("cS", 1))
            s_ssq = inc("cS")

            def _stats(e):
                e.drain()
                e.tensor_scalar_mul(st_s[:, 2:3], st_s[:, 0:1], 1.0 / H)
                e.drain()
                e.tensor_mul(st_s[:, 3:4], st_s[:, 2:3], st_s[:, 2:3])
                e.drain()
                e.scalar_tensor_tensor(out=st_s[:, 4:5], in0=st_s[:, 1:2],
                                       scalar=1.0 / H, in1=st_s[:, 3:4],
                                       op0=AOP.mult, op1=AOP.subtract)
                e.drain()
                return e.tensor_scalar_add(st_s[:, 7:8], st_s[:, 4:5], 1e-5)
            emit("vector", [("cS", s_ssq), ("cV", v_sx)], _stats, then=("cV", 1))
            v_var = inc("cV")
            emit("scalar", [("cV", v_var)],
                 lambda e: e.activation(out=st_s[:, 3:4], in_=st_s[:, 7:8],
                                        func=ACT.Sqrt),
                 then=("cS", 1))
            s_inv = inc("cS")

            def _invnb(e):
                e.drain()
                e.reciprocal(st_s[:, 5:6], st_s[:, 3:4])
                e.drain()
                return e.scalar_tensor_tensor(out=st_s[:, 6:7], in0=st_s[:, 2:3],
                                              scalar=-1.0, in1=st_s[:, 5:6],
                                              op0=AOP.mult, op1=AOP.mult)
            emit("vector", [("cS", s_inv)], _invnb, then=("cV", 1))
            v_nb = inc("cV")
            emit("scalar", [("cV", v_nb)],
                 lambda e, b=buf: e.activation(out=vA(o_xnst, H), in_=b,
                                               func=ACT.Identity,
                                               scale=st_s[:, 5:6],
                                               bias=st_s[:, 6:7]),
                 then=("cS", 1))
            c_xnact[tt] = inc("cS")
            for g in range(2):
                pv = psaA[:, (g % 2) * 1024:(g % 2 + 1) * 1024].rearrange(
                    "p (h c) -> p h c", h=8)
                wts = [("cS", c_xnact[tt]), ("cS", c_xnev.get((tt - 1, g), 0))]

                def _tr(e, g=g, pv=pv):
                    for h8 in range(8):
                        ht = g * 8 + h8
                        ins = e.transpose(pv[:, h8],
                                          vA(o_xnst, H)[:, ht * 128:(ht + 1) * 128],
                                          idf_s[:])
                    return ins
                emit("tensor", wts, _tr, then=("cT", 1))
                t_tr = inc("cT")
                emit("scalar", [("cT", t_tr)],
                     lambda e, g=g, pv=pv, tt=tt: e.activation(
                         out=xnT[:, g * 8:(g + 1) * 8, tt * 128:(tt + 1) * 128],
                         in_=pv, func=ACT.Copy),
                     then=("cS", 1))
                c_xnev[(tt, g)] = inc("cS")
        c_xnev_all = cnt["cS"]

        g_xnb = 0
        for j in range(NC):
            g_xnb = dma("gpsimd", "qG",
                        xn_b[j].rearrange("(q p) t -> q p t", p=128)
                        .transpose([1, 0, 2]),
                        xnT[:, NDT * j:NDT * (j + 1), :],
                        [("cS", c_xnev_all)] if j == 0 else [])
        emit("gpsimd", [("qG", g_xnb)],
             lambda e: e.collective_compute(
                 "AllToAll", AOP.bypass, replica_groups=[list(range(NC))],
                 ins=[xn_b.opt()], outs=[xng.opt()]),
             then=("cC", 1))
        cc_xn = inc("cC")

        s_xz_prev = 0
        for tch in range(T4):
            def _xz(e, tch=tch):
                for ht in range(HT):
                    ins = e.matmul(psaA[0:K96, 0:512], wx_b[:, ht, :],
                                   xnT[:, ht, tch * 512:(tch + 1) * 512],
                                   start=(ht == 0), stop=(ht == HT - 1))
                return ins
            emit("tensor", [("cS", c_xnev_all), ("cS", s_xz_prev)], _xz,
                 then=("cT", 1))
            t_xz = inc("cT")
            emit("scalar", [("cT", t_xz)],
                 lambda e, tch=tch: e.activation(
                     out=vA(o_xzst, T)[0:K96, tch * 512:(tch + 1) * 512],
                     in_=psaA[0:K96, 0:512], func=ACT.Copy),
                 then=("cS", 1))
            s_xz_prev = inc("cS")
        g_xz = dma("gpsimd", "qG", xz_b, vA(o_xzst, T)[0:K96, :],
                   [("cS", s_xz_prev)])
        emit("gpsimd", [("qG", g_xz)],
             lambda e: e.collective_compute(
                 "AllGather", AOP.bypass, replica_groups=[list(range(NC))],
                 ins=[xz_b.opt()], outs=[xzg.opt()]),
             then=("cC", 1))
        cc_xz = inc("cC")

        s_bcb = 0
        for j in range(NC):
            g_bcf = dma("gpsimd", "qG", vA(o_bcstf, T)[0:2 * NST, :],
                        xzg[j, RK:K96, :],
                        ([("cC", cc_xz)] if j == 0 else []) + [("cS", s_bcb)])
            emit("scalar", [("qG", g_bcf)],
                 lambda e: e.activation(out=vB(o_bcstb, T)[0:2 * NST, :],
                                        in_=vA(o_bcstf, T)[0:2 * NST, :],
                                        func=ACT.Copy),
                 then=("cS", 1))
            s_bcb = inc("cS")
            dma("gpsimd", "qG", bcD[:, j * T:(j + 1) * T],
                vB(o_bcstb, T)[0:2 * NST, :], [("cS", s_bcb)])

        s_mst = {}
        q_mst = {}
        for j in range(NC):
            wts = [("cS", s_mst.get((j - 2, T4 - 1, NDT - 1), 0))]
            if j == 0:
                wts.append(("cC", cc_xz))
            g_dr = dma("gpsimd", "qG", vA(o_drawf + (j % 2) * T, T)[0:RK, :],
                       xzg[j, 0:RK, :], wts)
            emit("scalar", [("qG", g_dr)],
                 lambda e, j=j: e.activation(
                     out=vB(o_drawb + (j % 2) * (T // 2), T)[0:RK, :],
                     in_=vA(o_drawf + (j % 2) * T, T)[0:RK, :], func=ACT.Copy),
                 then=("cS", 1))
            s_drb = inc("cS")
            for tc4 in range(T4):
                for dt in range(NDT):
                    kprev = (j, tc4 - 1, dt) if tc4 > 0 else (j - 1, T4 - 1, dt)
                    emit("tensor",
                         [("cS", s_drb), ("cS", c_prep),
                          ("cS", s_mst.get(kprev, 0)),
                          ("qG", q_mst.get(kprev, 0))],
                         lambda e, j=j, tc4=tc4, dt=dt: e.matmul(
                             psaA[:, dt * 512:(dt + 1) * 512],
                             wdt_b[:, dt * 128:(dt + 1) * 128],
                             vB(o_drawb + (j % 2) * (T // 2), T)[0:RK,
                                 tc4 * 512:(tc4 + 1) * 512],
                             start=True, stop=True),
                         then=("cT", 1))
                    t_dl = inc("cT")

                    def _m(e, dt=dt):
                        e.activation(out=vA(o_sgst, 512),
                                     in_=psaA[:, dt * 512:(dt + 1) * 512],
                                     func=ACT.Sigmoid, scale=-1.0,
                                     bias=nbdt[:, dt:dt + 1])
                        return e.activation(out=vA(o_mst + dt * 512, 512),
                                            in_=vA(o_sgst, 512), func=ACT.Ln)
                    emit("scalar", [("cT", t_dl)], _m, then=("cS", 1))
                    s_mv = inc("cS")
                    emit("vector", [("cS", s_mv)],
                         lambda e, dt=dt: e.tensor_scalar_min(
                             vA(o_mst + (NDT + dt) * 512, 512),
                             vA(o_mst + dt * 512, 512), 0.0),
                         then=("cV", 1))
                    v_mv = inc("cV")
                    q_mst[(j, tc4, dt)] = dma(
                        "gpsimd", "qG",
                        mD[dt, :, j * T + tc4 * 512: j * T + (tc4 + 1) * 512],
                        vA(o_mst + (NDT + dt) * 512, 512), [("cV", v_mv)])
                    s_mst[(j, tc4, dt)] = s_mv
        g_mD_all = cnt["qG"]
        c_delta_done = cnt["cS"]

        scan_yf = {}
        abar_c = {}
        qbc = {}
        ydma = {}
        fA_deq = {}
        fA_mm = {}
        fA_silu = {}
        fA_hactv = {}
        fA_hdma = {}

        def scan_chunk(ch):
            pp = ch % 2
            j = ch // CPB
            tloc = (ch % CPB) * TC
            glob = ch * TC
            wts = ([("cC", cc_xn), ("qG", g_mD_all)] if ch == 0 else
                   [("cV", scan_yf.get((ch - 2, NDT - 1), 0))])
            qC_ = dma(
                "gpsimd", "qG",
                vB(o_BC + (ch % 2) * (NST * TC), 2 * NST * TC).rearrange(
                    "p (n t) -> p n t", n=2 * NST),
                bcD[:, glob:glob + TC].partition_broadcast(128), wts)
            qm = {}
            qx = {}
            for dt in range(NDT):
                w2 = []
                if dt == 0:
                    w2 = ([("cC", cc_xn), ("qG", g_mD_all)] if ch == 0 else
                          [("cV", scan_yf.get((ch - 2, NDT - 1), 0))])
                qm[dt] = dma("scalar", "qC", vA(o_m + (dt * 2 + pp) * TC, TC),
                             mD[dt, :, glob:glob + TC], w2)
                qx[dt] = dma("scalar", "qC",
                             vB(o_xnsl + (dt * 2 + pp) * (TC // 2), TC),
                             xng[j, dt * 128:(dt + 1) * 128, tloc:tloc + TC])
            for dt in range(NDT):
                def _abar(e, ch=ch, dt=dt, pp=pp):
                    mv = vA(o_m + (dt * 2 + pp) * TC, TC)
                    ab = vB(o_Abar + ((ch * NDT + dt) % 4) * (NST * TC // 2),
                            NST * TC)
                    for n in range(NST):
                        ins = e.activation(out=ab[:, n * TC:(n + 1) * TC],
                                           in_=mv, func=ACT.Exp,
                                           scale=expA[:, dt * NST + n:
                                                      dt * NST + n + 1])
                    return ins
                emit("scalar",
                     [("qC", qm[dt]), ("cV", scan_yf.get((ch - 2, dt), 0))],
                     _abar, then=("cS", 1))
                abar_c[(ch, dt)] = inc("cS")

                def _vec(e, ch=ch, dt=dt, pp=pp):
                    mv = vA(o_m + (dt * 2 + pp) * TC, TC)
                    xnv = vB(o_xnsl + (dt * 2 + pp) * (TC // 2), TC)
                    dxv = vB(o_dx, TC)
                    ab = vB(o_Abar + ((ch * NDT + dt) % 4) * (NST * TC // 2),
                            NST * TC)
                    hb = vB(o_h + dt * (NG * TC // 2), NG * TC)
                    BC3 = vB(o_BC + (ch % 2) * (NST * TC), 2 * NST * TC).rearrange(
                        "p (n t) -> p n t", n=2 * NST)
                    bx = vB(o_Bx + dt * (NG * TC // 2), NG * TC)
                    zf = vB(o_zf, NST * TC)
                    yv = vA(o_y, TC)
                    e.scalar_tensor_tensor(out=dxv, in0=mv, scalar=-1.0,
                                           in1=xnv, op0=AOP.mult, op1=AOP.mult)
                    e.tensor_scalar_min(ab, ab, 1.0)
                    e.drain()
                    for ngr in range(NGR):
                        n0 = ngr * NG
                        e.tensor_tensor(
                            out=bx.rearrange("p (n t) -> p n t", n=NG),
                            in0=dxv.unsqueeze(1).broadcast_to([128, NG, TC]),
                            in1=BC3[:, n0:n0 + NG, :], op=AOP.mult)
                        e.nop()
                        for n in range(NG):
                            nn = n0 + n
                            e.tensor_tensor_scan(
                                out=hb[:, n * TC:(n + 1) * TC],
                                data0=ab[:, nn * TC:(nn + 1) * TC],
                                data1=bx[:, n * TC:(n + 1) * TC],
                                initial=init_s[:, dt, nn:nn + 1],
                                op0=AOP.mult, op1=AOP.add)
                            e.nop()
                        e.tensor_tensor(
                            out=zf[:, n0 * TC:(n0 + NG) * TC].rearrange(
                                "p (n t) -> p n t", n=NG),
                            in0=hb.rearrange("p (n t) -> p n t", n=NG),
                            in1=BC3[:, NST + n0:NST + n0 + NG, :],
                            op=AOP.mult)
                        e.tensor_copy(
                            init_s[:, dt, n0:n0 + NG],
                            hb.rearrange("p (n t) -> p n t", n=NG)[:, :, TC - 1])
                    z3 = zf.rearrange("p (n t) -> p n t", n=NST)
                    b3 = bx.rearrange("p (n t) -> p n t", n=NG)
                    e.drain()
                    e.tensor_tensor(out=b3, in0=z3[:, 0:8], in1=z3[:, 8:16],
                                    op=AOP.add)
                    e.nop()
                    e.tensor_tensor(out=b3[:, 0:4], in0=b3[:, 0:4],
                                    in1=b3[:, 4:8], op=AOP.add)
                    e.nop()
                    e.tensor_tensor(out=b3[:, 0:2], in0=b3[:, 0:2],
                                    in1=b3[:, 2:4], op=AOP.add)
                    e.nop()
                    e.tensor_tensor(out=yv, in0=b3[:, 0], in1=b3[:, 1],
                                    op=AOP.add)
                    e.nop()
                    return e.scalar_tensor_tensor(
                        out=vA(o_yf + (dt * 2 + pp) * TC, TC), in0=xnv,
                        scalar=dvec_s[:, dt:dt + 1], in1=yv,
                        op0=AOP.mult, op1=AOP.add)
                emit("vector",
                     [("cS", abar_c[(ch, dt)]), ("qC", qx[dt]), ("qG", qC_),
                      ("qG", ydma.get((ch - 2, dt), 0))],
                     _vec, then=("cV", 1))
                scan_yf[(ch, dt)] = inc("cV")
                ydma[(ch, dt)] = dma(
                    "gpsimd", "qG",
                    y_b[j, dt * 128:(dt + 1) * 128, tloc:tloc + TC],
                    vA(o_yf + (dt * 2 + pp) * TC, TC),
                    [("cV", scan_yf[(ch, dt)])])

        def ffnA_iter(it):
            wts = [("cS", fA_deq.get(it - 1, 0))]
            if it == 0:
                wts.append(("qS", q_wdpre))
            if it == 0:
                wts.append(("cS", c_delta_done))
            dma("sync", "qS",
                vI(o_gwI, HT * 128).rearrange("p (h c) -> p h c", h=HT),
                gwT_h[it], wts)
            qu = dma("sync", "qS",
                     vI(o_uwI, HT * 128).rearrange("p (h c) -> p h c", h=HT),
                     uwT_h[it])

            def _deq(e):
                e.activation(out=vB(o_gwB, HT * 128), in_=vI(o_gwI, HT * 128),
                             func=ACT.Copy, bias=-128.0)
                return e.activation(out=vB(o_uwB, HT * 128),
                                    in_=vI(o_uwI, HT * 128),
                                    func=ACT.Copy, bias=-128.0)
            emit("scalar", [("qS", qu), ("cT", fA_mm.get(it - 1, 0))], _deq,
                 then=("cS", 1))
            fA_deq[it] = inc("cS")

            def _mm(e):
                gw = vB(o_gwB, HT * 128).rearrange("p (h c) -> p h c", h=HT)
                uw = vB(o_uwB, HT * 128).rearrange("p (h c) -> p h c", h=HT)
                for ht in range(HT):
                    for tch in range(T4):
                        ins = e.matmul(psaA[:, tch * 512:(tch + 1) * 512],
                                       gw[:, ht],
                                       xnT[:, ht, tch * 512:(tch + 1) * 512],
                                       start=(ht == 0), stop=(ht == HT - 1))
                for ht in range(HT):
                    for tch in range(T4):
                        ins = e.matmul(psbA[:, tch * 512:(tch + 1) * 512],
                                       uw[:, ht],
                                       xnT[:, ht, tch * 512:(tch + 1) * 512],
                                       start=(ht == 0), stop=(ht == HT - 1))
                return ins
            emit("tensor",
                 [("cS", fA_deq[it]), ("cS", fA_silu.get(it - 1, 0)),
                  ("cV", fA_hactv.get(it - 1, 0))],
                 _mm, then=("cT", 1))
            fA_mm[it] = inc("cT")

            def _silu(e, it=it):
                for tch in range(T4):
                    ins = e.activation(
                        out=vB(o_silu, T)[:, tch * 512:(tch + 1) * 512],
                        in_=psaA[:, tch * 512:(tch + 1) * 512],
                        func=ACT.Silu, scale=gsc_s[:, it:it + 1])
                return ins
            emit("scalar", [("cT", fA_mm[it]), ("cV", fA_hactv.get(it - 1, 0))],
                 _silu, then=("cS", 1))
            fA_silu[it] = inc("cS")

            def _hact(e, it=it):
                for tch in range(T4):
                    ins = e.scalar_tensor_tensor(
                        out=vB(o_hact, T)[:, tch * 512:(tch + 1) * 512],
                        in0=psbA[:, tch * 512:(tch + 1) * 512],
                        scalar=usc_s[:, it:it + 1],
                        in1=vB(o_silu, T)[:, tch * 512:(tch + 1) * 512],
                        op0=AOP.mult, op1=AOP.mult)
                return ins
            emit("vector",
                 [("cS", fA_silu[it]), ("qS", fA_hdma.get(it - 1, 0))],
                 _hact, then=("cV", 1))
            fA_hactv[it] = inc("cV")
            fA_hdma[it] = dma("sync", "qS",
                              hactD[:, :, it, :].transpose([1, 0, 2]),
                              vB(o_hact, T).rearrange("p (a t) -> p a t", a=TT),
                              [("cV", fA_hactv[it])])

        for k in range(max(NCH, IT)):
            if k < NCH:
                scan_chunk(k)
            if k < IT and not no_ffn:
                ffnA_iter(k)

        emit("gpsimd", [("qG", ydma[(NCH - 1, NDT - 1)])],
             lambda e: e.collective_compute(
                 "AllToAll", AOP.bypass, replica_groups=[list(range(NC))],
                 ins=[y_b.opt()], outs=[yg.opt()]),
             then=("cC", 1))
        cc_y = inc("cC")
        c_scanA_done_V = cnt["cV"]
        c_scanA_done_S = cnt["cS"]

        ygf = yg.rearrange("c (q p) t -> p (c q) t", p=128)
        xTr = xT.rearrange("(s p) t -> p s t", p=128)
        outr = outT.rearrange("(s p) t -> p s t", p=128)
        fB_deq = {}
        fB_mm = {}
        fB_dn = {}
        fB_tr = {}
        fB_fin = {}
        fB_out = {}

        def fB_mm_lin(half, tt):
            if tt >= 0:
                return fB_mm.get((half, tt), 0)
            return fB_mm.get((half - 1, TT + tt), 0)

        for half in ([] if no_ffn else range(2)):
            wts = []
            if half == 0:
                wts += [("cV", c_scanA_done_V), ("cS", c_scanA_done_S),
                        ("qS", fA_hdma[IT - 1])]
            else:
                wts.append(("cT", fB_mm[(0, TT - 1)]))
            ITS = max(1, IT // 8)
            for sub in range(IT // ITS):
                qwd = dma("sync", "qS",
                          vB(o_wdH, IT * HLF).rearrange(
                              "p (i c) -> p i c",
                              i=IT)[:, sub * ITS:(sub + 1) * ITS],
                          wdBF.rearrange("(i p) c -> p i c", p=128)[:,
                              sub * ITS:(sub + 1) * ITS,
                              half * HLF:(half + 1) * HLF],
                          wts if sub == 0 else [])
            fB_deq[(half, IT - 1)] = 0
            fB_wload = qwd
            for tt in range(TT):
                pp = tt % 2
                wts = [("cT", fB_mm_lin(half, tt - 2))]
                if tt == 0:
                    wts.append(("qS", fB_wload))
                qh = dma("sync", "qS",
                         vB(o_hsb + pp * (IT * 128 // 2), IT * 128)
                         .rearrange("p (i t) -> p i t", i=IT),
                         hactD[tt], wts)
                qh_g = 0
                qxt = dma("sync", "qS",
                          vA(o_xTb, 8 * 128).rearrange("p (s t) -> p s t", s=8),
                          xTr[:, half * 8:(half + 1) * 8,
                              tt * 128:(tt + 1) * 128],
                          [("cV", fB_fin.get((half, tt - 1),
                                             fB_fin.get((half - 1, TT - 1), 0)))])
                wts = [("cV", fB_fin.get((half, tt - 1),
                                         fB_fin.get((half - 1, TT - 1), 0)))]
                if half == 0 and tt == 0:
                    wts.append(("cC", cc_y))
                qyr = dma("gpsimd", "qG",
                          vA(o_yrb, 8 * 128).rearrange("p (s t) -> p s t", s=8),
                          ygf[:, half * 8:(half + 1) * 8,
                              tt * 128:(tt + 1) * 128], wts)

                u = half * TT + tt

                def lin(d, k):
                    if tt - k >= 0:
                        return d.get((half, tt - k), 0)
                    return d.get((half - 1, TT + tt - k), 0)

                def _mmB(e, pp=pp, u=u):
                    hs = vB(o_hsb + pp * (IT * 128 // 2), IT * 128).rearrange(
                        "p (i t) -> p i t", i=IT)
                    wd = vB(o_wdH, IT * HLF).rearrange("p (i c) -> p i c", i=IT)
                    for it in range(IT):
                        for dch in range(2):
                            ins = e.matmul(
                                psaA[:, (u % 2) * 1024 + dch * 512:
                                     (u % 2) * 1024 + (dch + 1) * 512],
                                hs[:, it],
                                wd[:, it, dch * 512:(dch + 1) * 512],
                                start=(it == 0), stop=(it == IT - 1))
                    return ins
                emit("tensor",
                     [("qS", qh), ("qG", qh_g), ("cS", lin(fB_dn, 2))],
                     _mmB, then=("cT", 1))
                fB_mm[(half, tt)] = inc("cT")
                emit("scalar",
                     [("cT", fB_mm[(half, tt)]), ("cT", lin(fB_tr, 2))],
                     lambda e, u=u: e.activation(
                         out=vA(o_dnst + (u % 2) * HLF, HLF),
                         in_=psaA[:, (u % 2) * 1024:(u % 2) * 1024 + HLF],
                         func=ACT.Copy),
                     then=("cS", 1))
                fB_dn[(half, tt)] = inc("cS")

                def _trB(e, u=u):
                    for ds in range(8):
                        ins = e.transpose(
                            psbA[:, (u % 2) * 1024:(u % 2 + 1) * 1024]
                            .rearrange("p (s c) -> p s c", s=8)[:, ds],
                            vA(o_dnst + (u % 2) * HLF, HLF)[:,
                                ds * 128:(ds + 1) * 128], idf_s[:])
                    return ins
                emit("tensor",
                     [("cS", fB_dn[(half, tt)]), ("cV", lin(fB_fin, 2))],
                     _trB, then=("cT", 1))
                fB_tr[(half, tt)] = inc("cT")

                def _fin(e, half=half, u=u):
                    trv = (psbA[:, (u % 2) * 1024:(u % 2 + 1) * 1024]
                           .rearrange("p (s c) -> p s c", s=8))
                    xtv = vA(o_xTb, 8 * 128).rearrange("p (s t) -> p s t", s=8)
                    yrv = vA(o_yrb, 8 * 128).rearrange("p (s t) -> p s t", s=8)
                    fnv = vA(o_fin, 8 * 128).rearrange("p (s t) -> p s t", s=8)
                    ov = vA(o_outst, 8 * 128).rearrange("p (s t) -> p s t", s=8)
                    for ds in range(8):
                        dtg = half * 8 + ds
                        e.scalar_tensor_tensor(
                            out=fnv[:, ds], in0=trv[:, ds],
                            scalar=dsc_s[:, dtg:dtg + 1], in1=xtv[:, ds],
                            op0=AOP.mult, op1=AOP.add)
                    e.drain()
                    for ds in range(8):
                        ins = e.tensor_add(ov[:, ds], fnv[:, ds], yrv[:, ds])
                    return ins
                emit("vector",
                     [("cT", fB_tr[(half, tt)]), ("qS", qxt), ("qG", qyr),
                      ("qO", fB_out.get((half, tt - 1),
                                        fB_out.get((half - 1, TT - 1), 0)))],
                     _fin, then=("cV", 1))
                fB_fin[(half, tt)] = inc("cV")
                fB_out[(half, tt)] = dma(
                    "sync", "qO",
                    outr[:, half * 8:(half + 1) * 8, tt * 128:(tt + 1) * 128],
                    vA(o_outst, 8 * 128).rearrange("p (s t) -> p s t", s=8),
                    [("cV", fB_fin[(half, tt)])])
        if no_ffn:
            emit("gpsimd", [("cC", cc_y)], lambda e: None)
        if dbg:
            d_ea = nc.dram_tensor("d_expA", [128, NDT * NST], F32,
                                  kind="ExternalOutput").ap()
            dma("gpsimd", "qG", d_ea, expA[:],
                [("qO", cnt["qO"]), ("qG", cnt["qG"]), ("cV", cnt["cV"]),
                 ("cS", cnt["cS"])])
            taps = [("d_wdBF", wdBF),
                    ("d_xnb", xn_b), ("d_xzb", xz_b), ("d_yb", y_b),
                    ("d_xzg", xzg), ("d_mD", mD), ("d_xng", xng),
                    ("d_yg", yg), ("d_hact", hactD), ("d_bcD", bcD)]
            for nm, src_ap in taps:
                o = nc.dram_tensor(nm, list(src_ap.shape), src_ap.dtype,
                                   kind="ExternalOutput").ap()
                dma("gpsimd", "qG", o, src_ap, [])
            emit("gpsimd", [("qG", cnt["qG"])], lambda e: None)
        emit("sync", [("qO", cnt["qO"])], lambda e: None)

        with allow_ici_all_to_all():
            @block.sync
            def _(eng):
                for th in streams["sync"]:
                    th(eng)

            @block.vector
            def _(eng):
                for th in streams["vector"]:
                    th(eng)

            @block.scalar
            def _(eng):
                for th in streams["scalar"]:
                    th(eng)

            @block.tensor
            def _(eng):
                for th in streams["tensor"]:
                    th(eng)

            @block.gpsimd
            def _(eng):
                for th in streams["gpsimd"]:
                    th(eng)
    return nc


def _mk_inputs(cfg, L0_attn_x, L0_attn_W_x, L0_attn_W_dt, L0_attn_b_dt,
               L0_attn_A_log, L0_attn_D, L0_ffn_gate_w, L0_ffn_gate_scale,
               L0_ffn_up_w, L0_ffn_up_scale, L0_ffn_down_w, L0_ffn_down_scale):
    NC = cfg["NC"]; TF = cfg["TF"]; H = cfg["H"]; I = cfg["I"]
    NST = cfg["NST"]
    T = TF // NC
    DSL = H // NC
    NDT = DSL // 128
    IT = I // 128
    HT = H // 128
    f32 = np.float32
    x2 = np.asarray(L0_attn_x, f32).reshape(TF, H)
    wxT = np.ascontiguousarray(np.asarray(L0_attn_W_x, f32).T)
    wdtT_full = np.ascontiguousarray(np.asarray(L0_attn_W_dt, f32).T)
    b_dt = np.asarray(L0_attn_b_dt, f32)
    A_log = np.asarray(L0_attn_A_log, f32)
    Dv = np.asarray(L0_attn_D, f32)
    gwT = np.ascontiguousarray(
        np.asarray(L0_ffn_gate_w, np.int32).T.reshape(HT, 128, IT, 128)
        .transpose(2, 1, 0, 3))
    uwT = np.ascontiguousarray(
        np.asarray(L0_ffn_up_w, np.int32).T.reshape(HT, 128, IT, 128)
        .transpose(2, 1, 0, 3))
    dwT = np.ascontiguousarray(np.asarray(L0_ffn_down_w, np.int32).T)
    gsc2 = np.ascontiguousarray(np.asarray(L0_ffn_gate_scale, f32).reshape(IT, 128).T)
    usc2 = np.ascontiguousarray(np.asarray(L0_ffn_up_scale, f32).reshape(IT, 128).T)
    dsc2 = np.ascontiguousarray(np.asarray(L0_ffn_down_scale, f32).reshape(HT, 128).T)
    identb = np.eye(128, dtype=ml_dtypes.bfloat16)
    identf = np.eye(128, dtype=f32)
    in_maps = []
    for k in range(NC):
        sl = slice(k * DSL, (k + 1) * DSL)
        x_tok = np.ascontiguousarray(x2[k * T:(k + 1) * T])
        im = dict(
            x_tok=x_tok,
            xT=np.ascontiguousarray(x_tok.T),
            wxT=wxT,
            wdtT=np.ascontiguousarray(wdtT_full[:, sl]),
            bdt2=np.ascontiguousarray(b_dt[sl].reshape(NDT, 128).T),
            alog2=np.ascontiguousarray(
                A_log[sl].reshape(NDT, 128, NST).transpose(1, 0, 2)
                .reshape(128, NDT * NST)),
            dvec2=np.ascontiguousarray(Dv[sl].reshape(NDT, 128).T),
            gwT=gwT, uwT=uwT, dwT=dwT,
            gsc2=gsc2, usc2=usc2, dsc2=dsc2,
            identb=identb, identf=identf,
        )
        in_maps.append(im)
    return in_maps


def kernel(L0_attn_x, L0_attn_W_x, L0_attn_W_dt, L0_attn_b_dt, L0_attn_A_log,
           L0_attn_D, L0_ffn_gate_w, L0_ffn_gate_scale, L0_ffn_up_w,
           L0_ffn_up_scale, L0_ffn_down_w, L0_ffn_down_scale):
    cfg = CFG
    nc = build_nc(cfg)
    in_maps = _mk_inputs(cfg, L0_attn_x, L0_attn_W_x, L0_attn_W_dt,
                         L0_attn_b_dt, L0_attn_A_log, L0_attn_D,
                         L0_ffn_gate_w, L0_ffn_gate_scale, L0_ffn_up_w,
                         L0_ffn_up_scale, L0_ffn_down_w, L0_ffn_down_scale)
    with allow_ici_all_to_all():
        res = run_bass_kernel_spmd(nc, in_maps, core_ids=list(range(cfg["NC"])))
    TF = cfg["TF"]; H = cfg["H"]; NC = cfg["NC"]
    T = TF // NC
    out = np.empty((1, TF, H), np.float32)
    for k in range(NC):
        out[0, k * T:(k + 1) * T, :] = np.asarray(res.results[k]["outT"]).T
    return out
